# version 18
# baseline (speedup 1.0000x reference)
import numpy as np
import ml_dtypes
import concourse.bass as bass
import concourse.mybir as mybir
from concourse.bass_utils import run_bass_kernel_spmd
from contextlib import ExitStack

F32 = mybir.dt.float32
BF16 = mybir.dt.bfloat16
I32 = mybir.dt.int32
AF = mybir.ActivationFunctionType
OP = mybir.AluOpType

DBG = []
B_CORE = 4
T = 2048
D = 1024
NT = 8
C = 512
NCH = T // C
PIECE = 4096
NS = 3
PF = 2
EPS = 1e-5

KVR_BASE = 0
WOUT_BASE = 24576
FREC_BASE = 32768
FKEY_BASE = 40960
FVAL_BASE = 73728
FF_COLS = 106496
GU_BASE = 106496
DOWN_BASE = 155648
TC = 180224
NPIECE = TC // PIECE


def col_kvr(w, g, t):  return KVR_BASE + g * 3072 + w * 1024 + t * 128
def col_wout(g, t):    return WOUT_BASE + g * 1024 + t * 128
def col_frec(g, t):    return FREC_BASE + g * 1024 + t * 128
def col_fkey(h, t):    return FKEY_BASE + h * 1024 + t * 128
def col_fval(d, th):   return FVAL_BASE + d * 4096 + th * 128
def col_gu(w, h, t):   return GU_BASE + h * 2048 + w * 1024 + t * 128
def col_down(d, th):   return DOWN_BASE + d * 3072 + th * 128


class Buf:
    __slots__ = ("writer", "readers", "name")
    def __init__(self, name=""):
        self.writer = None
        self.readers = []
        self.name = name


class _Op:
    __slots__ = ("eng", "emit", "deps", "needs_inc", "val", "kind", "slot")


NDMASEM = 6


class Prog:
    ENGS = ("pe", "dve", "act", "pool", "sp")

    def __init__(self):
        self.streams = {e: [] for e in self.ENGS}
        self.dma_count = {e: 0 for e in self.ENGS}

    def _mk(self, eng, emit, kind, deps, reads, writes):
        o = _Op()
        o.eng = eng
        o.emit = emit
        o.kind = kind
        o.needs_inc = False
        o.val = 0
        d = set(deps)
        for b in reads:
            if b.writer is not None:
                d.add(b.writer)
            b.readers.append(o)
        for b in writes:
            if b.writer is not None:
                d.add(b.writer)
            for r in b.readers:
                d.add(r)
        for b in writes:
            b.writer = o
            b.readers = []
        d.discard(o)
        o.deps = list(d)
        self.streams[eng].append(o)
        return o

    def op(self, eng, emit, deps=(), reads=(), writes=()):
        return self._mk(eng, emit, "c", deps, reads, writes)

    def dma(self, queue, emit, deps=(), reads=(), writes=()):
        o = self._mk(queue, emit, "dma", deps, reads, writes)
        i = self.dma_count[queue]
        self.dma_count[queue] += 1
        o.slot = i % NDMASEM
        o.val = 16 * (i // NDMASEM + 1)
        return o

    def finalize(self, nc, block, esem, dsem):
        for e in self.ENGS:
            for o in self.streams[e]:
                for dep in o.deps:
                    if dep.kind == "c" and not (dep.eng == o.eng == "pe"):
                        dep.needs_inc = True
        for e in self.ENGS:
            cnt = 0
            for o in self.streams[e]:
                if o.kind == "c" and o.needs_inc:
                    cnt += 1
                    o.val = cnt
        waits = {}
        for e in self.ENGS:
            water = {}
            for o in self.streams[e]:
                w = {}
                for dep in o.deps:
                    if dep.kind == "dma":
                        key = ("d", dep.eng, dep.slot)
                    elif not (dep.eng == o.eng == "pe"):
                        key = ("e", dep.eng, 0)
                    else:
                        continue
                    if dep.val > w.get(key, 0):
                        w[key] = dep.val
                if o.kind == "dma" and o.val > 16:
                    key = ("d", o.eng, o.slot)
                    if o.val - 16 > w.get(key, 0):
                        w[key] = o.val - 16
                ww = []
                for key, v in w.items():
                    if v > water.get(key, 0):
                        water[key] = v
                        ww.append((key, v))
                waits[id(o)] = ww

        def emit_stream(ename):
            def run(eng):
                for o in self.streams[ename]:
                    for (kind, src, slot), v in waits[id(o)]:
                        sem = dsem[src][slot] if kind == "d" else esem[src]
                        eng.wait_ge(sem, v)
                    inst = o.emit(eng)
                    if o.kind == "dma":
                        inst.then_inc(dsem[o.eng][o.slot], 16)
                    elif o.needs_inc:
                        inst.then_inc(esem[o.eng], 1)
                if ename == "sp":
                    n = self.dma_count["sp"]
                    for s in range(min(n, NDMASEM)):
                        tot = (n - s + NDMASEM - 1) // NDMASEM
                        eng.wait_ge(dsem["sp"][s], 16 * tot)
            return run

        block.tensor(emit_stream("pe"))
        block.vector(emit_stream("dve"))
        block.scalar(emit_stream("act"))
        block.gpsimd(emit_stream("pool"))
        block.sync(emit_stream("sp"))


def build_nc():
    nc = bass.Bass("TRN2", target_bir_lowering=False)

    xd = nc.dram_tensor("xin", [B_CORE * D, T], F32, kind="ExternalInput").ap()
    yd = nc.dram_tensor("y", [B_CORE * D, T], F32, kind="ExternalOutput").ap()
    wtf = nc.dram_tensor("wtf", [128, FF_COLS], F32, kind="ExternalInput").ap()
    wti = nc.dram_tensor("wti", [128, TC - FF_COLS], I32, kind="ExternalInput").ap()
    cst_d = nc.dram_tensor("consts", [128, 88], F32, kind="ExternalInput").ap()
    cstr_d = nc.dram_tensor("cstr", [1, 2 * D], F32, kind="ExternalInput").ap()
    tapes = [nc.dram_tensor(f"wtape{j}", [128, PIECE], BF16, kind="Internal").ap()
             for j in range(NPIECE)]

    A = nc.alloc_sbuf_tensor

    XIN = [A(f"XIN{t}", [128, C], F32) for t in range(NT)]
    XRES = [A(f"XRES{t}", [128, C], F32) for t in range(NT)]
    XN = [A(f"XN{t}", [128, C + 1], BF16) for t in range(NT)]
    Y = [A(f"Y{t}", [128, C], F32) for t in range(NT)]
    RING = [A(f"RING{s}", [128, PIECE], BF16) for s in range(NS)]
    ARENA = [A(f"AR{h}", [128, C], BF16) for h in range(32)]
    MIXK = [A(f"MIXK{t}", [128, C], BF16) for t in range(NT)]
    MIXV = [A(f"MIXV{t}", [128, C], BF16) for t in range(NT)]
    MIXR = [A(f"MIXR{t}", [128, C], BF16) for t in range(NT)]
    SIG = [A(f"SIG{t}", [128, C], F32) for t in range(NT)]
    PT = [A(f"PT{i}", [128, C], F32) for i in range(2)]
    PVT = A("PVT", [128, C], F32)
    AL = A("AL", [128, C + 1], F32)
    BE = A("BE", [128, C + 1], F32)
    NUM = A("NUM", [128, C], F32)
    SC = [A(f"SC{i}", [128, C], F32) for i in range(2)]
    SRT = [A(f"SRT{i}", [128, C], F32) for i in range(2)]
    FTMP = [A(f"FTMP{i}", [128, C], F32) for i in range(2)]
    TMP1 = A("TMP1", [128, C], F32)
    TMPD = A("TMPD", [128, C], BF16)
    XB = [A(f"XB{i}", [128, C], BF16) for i in range(4)]
    XSQ = [A(f"XSQ{i}", [128, C], BF16) for i in range(4)]
    R_MU = A("R_MU", [1, C], F32)
    R_M2 = A("R_M2", [1, C], F32)
    R_MUSQ = A("R_MUSQ", [1, C], F32)
    R_S = A("R_S", [1, C], F32)
    R_B = A("R_B", [1, C], F32)
    R_HIF = A("R_HIF", [1, C], F32)
    R_HI = A("R_HI", [1, C], BF16)
    R_LO = A("R_LO", [1, C], BF16)
    CST = A("CST", [128, 88], F32)
    CDW8 = A("CDW8", [128, 8], F32)
    CDLAM = A("CDLAM", [128, 8], F32)
    CDEU = A("CDEU", [128, 8], F32)
    CDN = A("CDN", [128, 8], F32)
    CSTRB = A("CSTRB", [1, 2 * D], BF16)
    CARA = A("CARA", [128, NT], F32)
    CARB = A("CARB", [128, NT], F32)
    C1 = A("C1", [128, NT], F32)
    C2 = A("C2", [128, NT], F32)
    ONESC = A("ONESC", [128, 1], BF16)
    ONESR = A("ONESR", [1, 128], BF16)

    PS1 = nc.alloc_psum_tensor("PS1", [1, C], F32)
    PS2 = nc.alloc_psum_tensor("PS2", [1, C], F32)
    PA = nc.alloc_psum_tensor("PA", [128, C], F32)
    PB = nc.alloc_psum_tensor("PB", [128, C], F32)
    PM = [nc.alloc_psum_tensor(f"PM{i}", [128, C], F32) for i in range(4)]

    P = Prog()

    bXIN = [Buf(f"xin{t}") for t in range(NT)]
    bXRES = [Buf(f"xres{t}") for t in range(NT)]
    bXN = [Buf(f"xn{t}") for t in range(NT)]
    bARENA = [Buf(f"ar{h}") for h in range(32)]
    bMIXK = [Buf(f"mk{t}") for t in range(NT)]
    bMIXV = [Buf(f"mv{t}") for t in range(NT)]
    bMIXR = [Buf(f"mr{t}") for t in range(NT)]
    bSIG = [Buf(f"sig{t}") for t in range(NT)]
    bPT = [Buf("pt0"), Buf("pt1")]
    bPVT = Buf("pvt")
    bAL = Buf("al")
    bBE = Buf("be")
    bNUM = Buf("num")
    bSC = [Buf("sc0"), Buf("sc1")]
    bSRT = [Buf("srt0"), Buf("srt1")]
    bFTMP = [Buf("ft0"), Buf("ft1")]
    bTMP1 = Buf("tmp1")
    bTMPD = Buf("tmpd")
    bXB = [Buf(f"xb{i}") for i in range(4)]
    bXSQ = [Buf(f"xsq{i}") for i in range(4)]
    bMU, bM2, bMUSQ = Buf("mu"), Buf("m2"), Buf("musq")
    bSROW, bBROW, bHIF = Buf("srow"), Buf("brow"), Buf("hif")
    bSHI, bSLO, bBHI, bBLO = Buf("shi"), Buf("slo"), Buf("bhi"), Buf("blo")
    bCST = Buf("cst")
    bCDW, bCDN, bCDLAM, bCDEU = Buf("cdw"), Buf("cdn"), Buf("cdlam"), Buf("cdeu")
    bCARA, bCARB, bC1, bC2 = Buf("cara"), Buf("carb"), Buf("c1"), Buf("c2")
    bONESC, bONESR = Buf("onesc"), Buf("onesr")
    bCSTRB = Buf("cstrb")
    bY = [Buf(f"yy{t}") for t in range(NT)]
    bPS1, bPS2, bPA, bPB = Buf("ps1"), Buf("ps2"), Buf("pa"), Buf("pb")
    bPM = [Buf(f"pm{i}") for i in range(4)]
    bTAPE = [Buf(f"tape{j}") for j in range(NPIECE)]
    bSLOT = [Buf(f"slot{s}") for s in range(NS)]
    bYD = Buf("yd")

    def xin_ap(t):   return XIN[t][:]
    def xres_ap(t):  return XRES[t][:]
    def y_ap(t):     return Y[t][:]
    def ar_ap(h):    return ARENA[h][:]
    def mixk_ap(t):  return MIXK[t][:]
    def mixv_ap(t):  return MIXV[t][:]
    def mixr_ap(t):  return MIXR[t][:]
    def sig_ap(t):   return SIG[t][:]
    def xn_ap(t):    return XN[t][:, 1:C + 1]
    def lx_ap(t):    return XN[t][:, 0:C]
    def xn0_ap(t):   return XN[t][:, 0:1]
    def xnL_ap(t):   return XN[t][:, C:C + 1]
    def cst_col(base, t): return CST[:, base + t: base + t + 1]
    LN1W, LN1B, LN2W, LN2B, TDC, TFC = 0, 8, 16, 24, 32, 40
    TMK, TMV, TMR, FTMK, FTMR = 48, 56, 64, 72, 80
    def lam_col(t): return CDLAM[:, t: t + 1]
    def eu_col(t):  return CDEU[:, t: t + 1]

    dcst = P.dma("sp", lambda e: e.dma_start(out=CST[:], in_=cst_d), writes=[bCST])
    P.op("act", lambda e: e.activation(out=CDW8[:], in_=CST[:, TDC:TDC + 8],
                                       func=AF.Exp, bias=0.0, scale=1.0),
         reads=[bCST], writes=[bCDW])
    P.op("dve", lambda e: e.tensor_scalar(out=CDN[:], in0=CDW8[:], scalar1=-1.0,
                                          scalar2=None, op0=OP.mult),
         reads=[bCDW], writes=[bCDN])
    P.op("act", lambda e: e.activation(out=CDLAM[:], in_=CDN[:],
                                       func=AF.Exp, bias=0.0, scale=1.0),
         reads=[bCDN], writes=[bCDLAM])
    P.op("act", lambda e: e.activation(out=CDEU[:], in_=CST[:, TFC:TFC + 8],
                                       func=AF.Exp, bias=0.0, scale=1.0),
         reads=[bCST], writes=[bCDEU])
    P.dma("pool", lambda e: e.dma_start(out=CSTRB[:], in_=cstr_d), writes=[bCSTRB])
    P.op("dve", lambda e: e.memset(ONESC[:], 1.0), writes=[bONESC])
    P.op("dve", lambda e: e.memset(ONESR[:], 1.0), writes=[bONESR])

    NPF = FF_COLS // PIECE
    for j in range(NPIECE):
        if j < NPF:
            src = wtf[:, j * PIECE:(j + 1) * PIECE]
        else:
            src = wti[:, (j - NPF) * PIECE:(j - NPF + 1) * PIECE]
        dst = tapes[j][:]
        P.dma("pool", (lambda s=src, d=dst: (lambda e: e.dma_start(out=d, in_=s)))(),
              writes=[bTAPE[j]])


    tapped = set()

    def tap(label, ap_or_parts, bufs=None):
        if label not in DBG or label in tapped:
            return
        tapped.add(label)
        if bufs is not None:
            parts = [(ap_or_parts, list(bufs))]
        else:
            parts = [(a, list(bb)) for a, bb in ap_or_parts]
        tot = sum(a.shape[1] for a, _ in parts)
        td = nc.dram_tensor(f"tap_{label}", [parts[0][0].shape[0], tot],
                            parts[0][0].dtype, kind="ExternalOutput").ap()
        off = 0
        for a, bb in parts:
            d = td[:, off:off + a.shape[1]]
            off += a.shape[1]
            P.dma("sp", (lambda a_=a, d_=d: (lambda e: e.dma_start(out=d_, in_=a_)))(),
                  reads=bb)

    tap("cst", CST[:], [bCST])
    tap("tape0", tapes[0][:], [bTAPE[0]])
    tap("tape26", tapes[26][:], [bTAPE[26]])

    ring_emitted = {}

    def ensure_ring(g, j):
        for jj in range(j, min(j + PF + 1, NPIECE)):
            if (g, jj) in ring_emitted:
                continue
            ring_emitted[(g, jj)] = True
            gp = g * NPIECE + jj
            s = gp % NS
            dst = RING[s][:]
            src = tapes[jj][:]
            P.dma("sp", (lambda s_=src, d_=dst: (lambda e: e.dma_start(out=d_, in_=s_)))(),
                  reads=[bTAPE[jj]], writes=[bSLOT[s]])

    def ring_lhsT(g, col):
        j = col // PIECE
        gp = g * NPIECE + j
        s = gp % NS
        off = col % PIECE
        return RING[s][:, off: off + 128], bSLOT[s]

    def mm(g, ps, ps_buf, col, rhs_ap, rhs_buf, start, stop):
        ensure_ring(g, col // PIECE)
        la, sbuf_ = ring_lhsT(g, col)
        reads = [sbuf_, rhs_buf]
        writes = [ps_buf] if (start or stop) else []
        if not start:
            reads = reads + [ps_buf]
        return P.op("pe",
                    (lambda la_=la, r_=rhs_ap, p_=ps, st=start, sp_=stop:
                     (lambda e: e.matmul(p_[:], la_, r_, start=st, stop=sp_,
                                         skip_group_check=True)))(),
                    reads=reads, writes=writes)

    def ln_stat_cast(t, src_one, src_buf):
        r = t % 4
        P.op("dve", (lambda r_=r, s_=src_one: (lambda e: e.tensor_copy(
            out=XB[r_][:], in_=s_)))(),
             reads=[src_buf], writes=[bXB[r]])
        P.op("act", (lambda r_=r, s_=src_one: (lambda e: e.activation(
            out=XSQ[r_][:], in_=s_, func=AF.Square, bias=0.0, scale=1.0)))(),
             reads=[src_buf], writes=[bXSQ[r]])

    def ln_stat_mm(t):
        r = t % 4
        P.op("pe", (lambda t_=t, r_=r: (lambda e: e.matmul(
            PS1[:], ONESC[:], XB[r_][:], start=(t_ == 0), stop=(t_ == NT - 1),
            skip_group_check=True)))(),
             reads=[bXB[r], bONESC] + ([bPS1] if t else []),
             writes=[bPS1] if (t == 0 or t == NT - 1) else [])
        P.op("pe", (lambda t_=t, r_=r: (lambda e: e.matmul(
            PS2[:], ONESC[:], XSQ[r_][:], start=(t_ == 0), stop=(t_ == NT - 1),
            skip_group_check=True)))(),
             reads=[bXSQ[r], bONESC] + ([bPS2] if t else []),
             writes=[bPS2] if (t == 0 or t == NT - 1) else [])

    def ln_rows():
        MU, M2, MUSQ = R_MU[:], R_M2[:], R_MUSQ[:]
        SROW, BROW = R_S[:], R_B[:]
        P.op("dve", lambda e: e.tensor_scalar(out=MU, in0=PS1[:], scalar1=1.0 / D,
                                              scalar2=None, op0=OP.mult),
             reads=[bPS1], writes=[bMU])
        P.op("act", lambda e: e.activation(out=M2, in_=PS2[:], func=AF.Identity,
                                           bias=0.0, scale=1.0 / D),
             reads=[bPS2], writes=[bM2])
        P.op("dve", lambda e: e.tensor_tensor(out=MUSQ, in0=MU, in1=MU, op=OP.mult),
             reads=[bMU], writes=[bMUSQ])
        P.op("dve", lambda e: e.scalar_tensor_tensor(out=M2, in0=M2, scalar=EPS,
                                                     in1=MUSQ, op0=OP.add,
                                                     op1=OP.subtract),
             reads=[bM2, bMUSQ], writes=[bM2])
        P.op("act", lambda e: e.activation(out=M2, in_=M2, func=AF.Ln,
                                           bias=0.0, scale=1.0),
             reads=[bM2], writes=[bM2])
        P.op("act", lambda e: e.activation(out=SROW, in_=M2, func=AF.Exp,
                                           bias=0.0, scale=-0.5),
             reads=[bM2], writes=[bSROW])
        P.op("dve", lambda e: e.tensor_copy(out=R_HI[:], in_=SROW),
             reads=[bSROW], writes=[bSHI])
        P.op("dve", lambda e: e.tensor_tensor(out=BROW, in0=MU, in1=SROW, op=OP.mult),
             reads=[bMU, bSROW], writes=[bBROW])
        P.op("dve", lambda e: e.tensor_copy(out=R_LO[:], in_=BROW),
             reads=[bBROW], writes=[bSLO])

    def outer_bcast():
        P.op("pe", lambda e: e.matmul(PA[:], ONESR[:], R_HI[:], start=True,
                                      stop=True, skip_group_check=True),
             reads=[bSHI, bONESR], writes=[bPA])
        P.op("pe", lambda e: e.matmul(PB[:], ONESR[:], R_LO[:], start=True,
                                      stop=True, skip_group_check=True),
             reads=[bSLO, bONESR], writes=[bPB])

    PAIR_AB = (((PA, PB), (bPA, bPB)),)
    PAIR_AB23 = (((PA, PB), (bPA, bPB)), ((PM[2], PM[3]), (bPM[2], bPM[3])))
    PAIR_3 = (((PA, PB), (bPA, bPB)), ((PM[2], PM[3]), (bPM[2], bPM[3])),
              ((PM[0], PM[1]), (bPM[0], bPM[1])))

    def apply_tile(t, ci_this, wbase, bbase, src_ap_, src_bufs, carry, carrybuf,
                   pairs):
        pa_t, pb_t = pairs[t % len(pairs)][0]
        bpa_t, bpb_t = pairs[t % len(pairs)][1]
        if True:
            wrow = CSTRB[0:1, wbase + t * 128: wbase + (t + 1) * 128]
            P.op("pe", (lambda w_=wrow, p_=pa_t: (lambda e: e.matmul(
                p_[:], w_, R_HI[:], start=True, stop=True,
                skip_group_check=True)))(),
                 reads=[bSHI, bCSTRB], writes=[bpa_t])
            P.op("pe", (lambda w_=wrow, p_=pb_t: (lambda e: e.matmul(
                p_[:], w_, R_LO[:], start=True, stop=True,
                skip_group_check=True)))(),
                 reads=[bSLO, bCSTRB], writes=[bpb_t])
            if ci_this == 0:
                P.op("dve", (lambda t_=t: (lambda e: e.memset(xn0_ap(t_), 0.0)))(),
                     writes=[bXN[t]])
            else:
                P.op("dve", (lambda t_=t, cb=carry: (lambda e: e.tensor_copy(
                    out=xn0_ap(t_), in_=cb[:, t_:t_ + 1])))(),
                     reads=[carrybuf], writes=[bXN[t]])
            P.op("dve", (lambda t_=t, p_=pa_t, bp=bpa_t: (lambda e: e.tensor_tensor(
                out=TMP1[:], in0=src_ap_(t_), in1=p_[:], op=OP.mult)))(),
                 reads=[src_bufs[t], bpa_t], writes=[bTMP1])
            P.op("dve", (lambda t_=t, p_=pb_t, bb=bbase: (lambda e: e.scalar_tensor_tensor(
                out=xn_ap(t_), in0=TMP1[:], scalar=cst_col(bb, t_), in1=p_[:],
                op0=OP.add, op1=OP.subtract)))(),
                 reads=[bTMP1, bpb_t, bCST], writes=[bXN[t]])

    def mix_tile(t, mixes):
        P.op("dve", (lambda t_=t: (lambda e: e.tensor_tensor(
            out=TMPD[:], in0=xn_ap(t_), in1=lx_ap(t_), op=OP.subtract)))(),
             reads=[bXN[t]], writes=[bTMPD])
        for base, MAP, mb in mixes:
            P.op("dve", (lambda t_=t, b_=base, m_=MAP: (lambda e: e.scalar_tensor_tensor(
                out=m_(t_), in0=TMPD[:], scalar=cst_col(b_, t_), in1=lx_ap(t_),
                op0=OP.mult, op1=OP.add)))(),
                 reads=[bTMPD, bXN[t], bCST], writes=[mb[t]])

    def carry_saves(carry, carrybuf):
        for t in range(NT):
            P.op("dve", (lambda t_=t, cb=carry: (lambda e: e.tensor_copy(
                out=cb[:, t_:t_ + 1], in_=xnL_ap(t_))))(),
                 reads=[bXN[t]], writes=[carrybuf])

    def inner_apply(ci_this, wbase, bbase, src_ap_, src_bufs, carry, carrybuf,
                    pairs):
        for t in range(NT):
            apply_tile(t, ci_this, wbase, bbase, src_ap_, src_bufs, carry,
                       carrybuf, pairs)

    def inner_mixes(mixes, carry, carrybuf):
        for t in range(NT):
            mix_tile(t, mixes)
        carry_saves(carry, carrybuf)

    xd_r = xd.rearrange("(b t p) c -> b t p c", b=B_CORE, t=NT, p=128)
    yd_r = yd.rearrange("(b t p) c -> b t p c", b=B_CORE, t=NT, p=128)

    def emit_x_load(gq):
        bq, ciq = gq // NCH, gq % NCH
        for t in range(NT):
            srcx = xd_r[bq][t][:, ciq * C: ciq * C + C]
            P.dma("sp", (lambda s_=srcx, t_=t: (lambda e: e.dma_start(
                out=XIN[t_][:], in_=s_)))(),
                  writes=[bXIN[t]])

    LN1_MIXES = ((TMK, mixk_ap, bMIXK), (TMV, mixv_ap, bMIXV),
                 (TMR, mixr_ap, bMIXR))
    LN2_MIXES = ((FTMK, mixv_ap, bMIXV), (FTMR, mixr_ap, bMIXR))

    emit_x_load(0)
    for t in range(NT):
        ln_stat_cast(t, xin_ap(t), bXIN[t])
        ln_stat_mm(t)
    ln_rows()
    inner_apply(0, 0, LN1B, xin_ap, bXIN, C1, bC1, PAIR_AB23)
    inner_mixes(LN1_MIXES, C1, bC1)

    NG = B_CORE * NCH
    pending_ln4 = [None]

    for g in range(NG):
        b, ci = g // NCH, g % NCH
        c0 = ci * C

        for t in range(NT):
            kb, bkb = PM[t % 2], bPM[t % 2]
            vb, bvb = (PM[2], bPM[2]) if t % 2 == 0 else (PA, bPA)
            rb, brb = (PM[3], bPM[3]) if t % 2 == 0 else (PB, bPB)
            for kt in range(NT):
                mm(g, kb, bkb, col_kvr(0, t, kt), mixk_ap(kt), bMIXK[kt],
                   kt == 0, kt == NT - 1)
            for kt in range(NT):
                mm(g, vb, bvb, col_kvr(1, t, kt), mixv_ap(kt), bMIXV[kt],
                   kt == 0, kt == NT - 1)
            for kt in range(NT):
                mm(g, rb, brb, col_kvr(2, t, kt), mixr_ap(kt), bMIXR[kt],
                   kt == 0, kt == NT - 1)
            if t == 2 and pending_ln4[0] is not None:
                pending_ln4[0]()
                pending_ln4[0] = None
            r = t % 2
            P.op("act", (lambda r_=r, k_=kb: (lambda e: e.activation(
                out=PT[r_][:], in_=k_[:], func=AF.Exp, bias=0.0, scale=1.0)))(),
                 reads=[bkb], writes=[bPT[r]])
            P.op("dve", (lambda r_=r, v_=vb: (lambda e: e.tensor_tensor(
                out=PVT[:], in0=PT[r_][:], in1=v_[:], op=OP.mult)))(),
                 reads=[bPT[r], bvb], writes=[bPVT])
            P.op("act", (lambda r_=r, rr=rb: (lambda e: e.activation(
                out=SRT[r_][:], in_=rr[:], func=AF.Exp, bias=0.0, scale=-1.0)))(),
                 reads=[brb], writes=[bSRT[r]])
            ia = 0.0 if ci == 0 else CARA[:, t:t + 1]
            ib = 0.0 if ci == 0 else CARB[:, t:t + 1]
            car_reads = [] if ci == 0 else [bCARA]
            cbr_reads = [] if ci == 0 else [bCARB]
            P.op("dve", (lambda t_=t, i_=ia: (lambda e: e.tensor_tensor_scan(
                out=AL[:, 1:C + 1], data0=lam_col(t_).broadcast_to([128, C]),
                data1=PVT[:], initial=i_, op0=OP.mult, op1=OP.add)))(),
                 reads=[bPVT, bCDLAM] + car_reads, writes=[bAL])
            P.op("dve", (lambda t_=t, r_=r, i_=ib: (lambda e: e.tensor_tensor_scan(
                out=BE[:, 1:C + 1], data0=lam_col(t_).broadcast_to([128, C]),
                data1=PT[r_][:], initial=i_, op0=OP.mult, op1=OP.add)))(),
                 reads=[bPT[r], bCDLAM] + cbr_reads, writes=[bBE])
            if ci == 0:
                P.op("dve", (lambda t_=t: (lambda e: e.tensor_scalar(
                    out=NUM[:, 0:1], in0=PVT[:, 0:1], scalar1=eu_col(t_),
                    scalar2=None, op0=OP.mult)))(),
                     reads=[bPVT, bCDEU], writes=[bNUM])
            else:
                P.op("dve", (lambda t_=t: (lambda e: e.scalar_tensor_tensor(
                    out=NUM[:, 0:1], in0=PVT[:, 0:1], scalar=eu_col(t_),
                    in1=CARA[:, t_:t_ + 1], op0=OP.mult, op1=OP.add)))(),
                     reads=[bPVT, bCDEU, bCARA], writes=[bNUM])
            P.op("dve", (lambda t_=t: (lambda e: e.scalar_tensor_tensor(
                out=NUM[:, 1:C], in0=PVT[:, 1:C], scalar=eu_col(t_),
                in1=AL[:, 1:C], op0=OP.mult, op1=OP.add)))(),
                 reads=[bPVT, bCDEU, bAL], writes=[bNUM])
            if ci == 0:
                P.op("dve", (lambda t_=t, r_=r: (lambda e: e.tensor_scalar(
                    out=SC[r_][:, 0:1], in0=PT[r_][:, 0:1], scalar1=eu_col(t_),
                    scalar2=None, op0=OP.mult)))(),
                     reads=[bPT[r], bCDEU], writes=[bSC[r]])
            else:
                P.op("dve", (lambda t_=t, r_=r: (lambda e: e.scalar_tensor_tensor(
                    out=SC[r_][:, 0:1], in0=PT[r_][:, 0:1], scalar=eu_col(t_),
                    in1=CARB[:, t_:t_ + 1], op0=OP.mult, op1=OP.add)))(),
                     reads=[bPT[r], bCDEU, bCARB], writes=[bSC[r]])
            P.op("dve", (lambda t_=t, r_=r: (lambda e: e.scalar_tensor_tensor(
                out=SC[r_][:, 1:C], in0=PT[r_][:, 1:C], scalar=eu_col(t_),
                in1=BE[:, 1:C], op0=OP.mult, op1=OP.add)))(),
                 reads=[bPT[r], bCDEU, bBE], writes=[bSC[r]])
            P.op("dve", (lambda t_=t: (lambda e: e.tensor_copy(
                out=CARA[:, t_:t_ + 1], in_=AL[:, C:C + 1])))(),
                 reads=[bAL], writes=[bCARA])
            P.op("dve", (lambda t_=t: (lambda e: e.tensor_copy(
                out=CARB[:, t_:t_ + 1], in_=BE[:, C:C + 1])))(),
                 reads=[bBE], writes=[bCARB])
            P.op("act", (lambda r_=r: (lambda e: e.activation(
                out=SC[r_][:], in_=SC[r_][:], func=AF.Ln, bias=0.0, scale=1.0)))(),
                 reads=[bSC[r]], writes=[bSC[r]])
            P.op("act", (lambda r_=r: (lambda e: e.activation(
                out=SC[r_][:], in_=SC[r_][:], func=AF.Exp, bias=0.0, scale=-1.0)))(),
                 reads=[bSC[r]], writes=[bSC[r]])
            P.op("act", (lambda r_=r: (lambda e: e.activation(
                out=SRT[r_][:], in_=SRT[r_][:], func=AF.Ln, bias=1.0, scale=1.0)))(),
                 reads=[bSRT[r]], writes=[bSRT[r]])
            P.op("act", (lambda r_=r: (lambda e: e.activation(
                out=SRT[r_][:], in_=SRT[r_][:], func=AF.Exp, bias=0.0, scale=-1.0)))(),
                 reads=[bSRT[r]], writes=[bSRT[r]])
            P.op("dve", (lambda r_=r: (lambda e: e.tensor_tensor(
                out=NUM[:], in0=NUM[:], in1=SC[r_][:], op=OP.mult)))(),
                 reads=[bNUM, bSC[r]], writes=[bNUM])
            P.op("dve", (lambda t_=t, r_=r: (lambda e: e.tensor_tensor(
                out=ar_ap(t_), in0=NUM[:], in1=SRT[r_][:], op=OP.mult)))(),
                 reads=[bNUM, bSRT[r]], writes=[bARENA[t]])

        for gg in range(NT):
            ps, bps = PM[gg % 4], bPM[gg % 4]
            for kt in range(NT):
                mm(g, ps, bps, col_wout(gg, kt), ar_ap(kt), bARENA[kt],
                   kt == 0, kt == NT - 1)
            P.op("dve", (lambda g_=gg, p_=ps: (lambda e: e.tensor_tensor(
                out=xres_ap(g_), in0=xin_ap(g_), in1=p_[:], op=OP.add)))(),
                 reads=[bXIN[gg], bps], writes=[bXRES[gg]])
            ln_stat_cast(gg, xres_ap(gg), bXRES[gg])
            if gg >= 2:
                ln_stat_mm(gg - 2)
        ln_stat_mm(NT - 2)
        ln_stat_mm(NT - 1)

        ln_rows()
        inner_apply(ci, D, LN2B, xres_ap, bXRES, C2, bC2, PAIR_3)
        inner_mixes(LN2_MIXES, C2, bC2)

        for gg in range(NT):
            ps, bps = PM[2 + gg % 2], bPM[2 + gg % 2]
            for kt in range(NT):
                mm(g, ps, bps, col_frec(gg, kt), mixr_ap(kt), bMIXR[kt],
                   kt == 0, kt == NT - 1)
            P.op("act", (lambda g_=gg, p_=ps: (lambda e: e.activation(
                out=sig_ap(g_), in_=p_[:], func=AF.Exp, bias=0.0, scale=-1.0)))(),
                 reads=[bps], writes=[bSIG[gg]])

        for h in range(32):
            ps, bps = PM[h % 4], bPM[h % 4]
            for kt in range(NT):
                mm(g, ps, bps, col_fkey(h, kt), mixv_ap(kt), bMIXV[kt],
                   kt == 0, kt == NT - 1)
            P.op("act", (lambda h_=h, p_=ps: (lambda e: e.activation(
                out=ar_ap(h_), in_=p_[:], func=AF.Relu, bias=0.0, scale=1.0)))(),
                 reads=[bps], writes=[bARENA[h]])
            P.op("dve", (lambda h_=h: (lambda e: e.tensor_tensor(
                out=ar_ap(h_), in0=ar_ap(h_), in1=ar_ap(h_), op=OP.mult)))(),
                 reads=[bARENA[h]], writes=[bARENA[h]])
            if h < 2 * NT and h % 2 == 1:
                gg2 = h // 2
                P.op("act", (lambda g_=gg2: (lambda e: e.activation(
                    out=sig_ap(g_), in_=sig_ap(g_), func=AF.Ln, bias=1.0, scale=1.0)))(),
                     reads=[bSIG[gg2]], writes=[bSIG[gg2]])
                P.op("act", (lambda g_=gg2: (lambda e: e.activation(
                    out=sig_ap(g_), in_=sig_ap(g_), func=AF.Exp, bias=0.0, scale=-1.0)))(),
                     reads=[bSIG[gg2]], writes=[bSIG[gg2]])

        for dd in range(NT):
            ps, bps = PM[dd % 4], bPM[dd % 4]
            for th in range(32):
                mm(g, ps, bps, col_fval(dd, th), ar_ap(th), bARENA[th],
                   th == 0, th == 31)
            P.op("dve", (lambda d_=dd, p_=ps: (lambda e: e.tensor_tensor(
                out=TMP1[:], in0=sig_ap(d_), in1=p_[:], op=OP.mult)))(),
                 reads=[bSIG[dd], bps], writes=[bTMP1])
            P.op("dve", (lambda d_=dd: (lambda e: e.tensor_tensor(
                out=xres_ap(d_), in0=xres_ap(d_), in1=TMP1[:], op=OP.add)))(),
                 reads=[bXRES[dd], bTMP1], writes=[bXRES[dd]])
            P.op("dve", (lambda t_=dd: (lambda e: e.tensor_tensor(
                out=xres_ap(t_), in0=xres_ap(t_), in1=xin_ap(t_), op=OP.add)))(),
                 reads=[bXRES[dd], bXIN[dd]], writes=[bXRES[dd]])
            ln_stat_cast(dd, xres_ap(dd), bXRES[dd])
            if dd >= 2:
                ln_stat_mm(dd - 2)
        ln_stat_mm(NT - 2)
        ln_stat_mm(NT - 1)

        ln_rows()
        outer_bcast()
        for t in range(NT):
            P.op("dve", (lambda t_=t: (lambda e: e.tensor_tensor(
                out=TMP1[:], in0=xres_ap(t_), in1=PA[:], op=OP.mult)))(),
                 reads=[bXRES[t], bPA], writes=[bTMP1])
            P.op("dve", (lambda t_=t: (lambda e: e.tensor_tensor(
                out=y_ap(t_), in0=TMP1[:], in1=PB[:], op=OP.subtract)))(),
                 reads=[bTMP1, bPB], writes=[bY[t]])
            P.op("dve", (lambda t_=t: (lambda e: e.tensor_copy(
                out=ar_ap(24 + t_), in_=y_ap(t_))))(),
                 reads=[bY[t]], writes=[bARENA[24 + t]])

        if g + 1 < NG:
            emit_x_load(g + 1)

        for h in range(24):
            gp, bgp = PM[h % 2], bPM[h % 2]
            up, bup = PM[2 + h % 2], bPM[2 + h % 2]
            for kt in range(NT):
                mm(g, gp, bgp, col_gu(0, h, kt), ar_ap(24 + kt), bARENA[24 + kt],
                   kt == 0, kt == NT - 1)
            for kt in range(NT):
                mm(g, up, bup, col_gu(1, h, kt), ar_ap(24 + kt), bARENA[24 + kt],
                   kt == 0, kt == NT - 1)
            r = h % 2
            P.op("dve", (lambda r_=r, p_=gp: (lambda e: e.tensor_scalar_max(
                out=FTMP[r_][:], in0=p_[:], scalar1=-5000.0)))(),
                 reads=[bgp], writes=[bFTMP[r]])
            P.op("act", (lambda r_=r: (lambda e: e.activation(
                out=FTMP[r_][:], in_=FTMP[r_][:], func=AF.Exp, bias=0.0,
                scale=-1.0 / 127.0)))(),
                 reads=[bFTMP[r]], writes=[bFTMP[r]])
            P.op("act", (lambda r_=r: (lambda e: e.activation(
                out=FTMP[r_][:], in_=FTMP[r_][:], func=AF.Ln, bias=1.0, scale=1.0)))(),
                 reads=[bFTMP[r]], writes=[bFTMP[r]])
            P.op("act", (lambda r_=r: (lambda e: e.activation(
                out=FTMP[r_][:], in_=FTMP[r_][:], func=AF.Exp, bias=0.0, scale=-1.0)))(),
                 reads=[bFTMP[r]], writes=[bFTMP[r]])
            P.op("dve", (lambda r_=r, p_=gp: (lambda e: e.tensor_tensor(
                out=FTMP[r_][:], in0=FTMP[r_][:], in1=p_[:], op=OP.mult)))(),
                 reads=[bFTMP[r], bgp], writes=[bFTMP[r]])
            P.op("dve", (lambda r_=r, h_=h, p_=up: (lambda e: e.scalar_tensor_tensor(
                out=ar_ap(h_), in0=FTMP[r_][:], scalar=1.0 / (127.0 * 127.0),
                in1=p_[:], op0=OP.mult, op1=OP.mult)))(),
                 reads=[bFTMP[r], bup], writes=[bARENA[h]])
            if g + 1 < NG and 8 <= h < 16:
                ln_stat_cast(h - 8, xin_ap(h - 8), bXIN[h - 8])
            if g + 1 < NG and 11 <= h < 19:
                ln_stat_mm(h - 11)
        if g + 1 < NG:
            ln_rows()

        for dd in range(NT):
            ps, bps = PM[dd % 4], bPM[dd % 4]
            for th in range(24):
                mm(g, ps, bps, col_down(dd, th), ar_ap(th), bARENA[th],
                   th == 0, th == 23)
            P.op("dve", (lambda d_=dd, p_=ps: (lambda e: e.scalar_tensor_tensor(
                out=xres_ap(d_), in0=p_[:], scalar=1.0 / 127.0, in1=y_ap(d_),
                op0=OP.mult, op1=OP.add)))(),
                 reads=[bY[dd], bps], writes=[bXRES[dd]])
            ln_stat_cast(dd, xres_ap(dd), bXRES[dd])
            if dd >= 2:
                ln_stat_mm(dd - 2)
            if g + 1 < NG:
                apply_tile(dd, (g + 1) % NCH, 0, LN1B, xin_ap, bXIN, C1, bC1,
                           PAIR_AB)
                mix_tile(dd, LN1_MIXES)
        ln_stat_mm(NT - 2)
        ln_stat_mm(NT - 1)
        if g + 1 < NG:
            carry_saves(C1, bC1)
        ln_rows()

        def ln4_finish(b_=b, c0_=c0):
            outer_bcast()
            for t in range(NT):
                P.op("dve", (lambda t_=t: (lambda e: e.tensor_tensor(
                    out=TMP1[:], in0=xres_ap(t_), in1=PA[:], op=OP.mult)))(),
                     reads=[bXRES[t], bPA], writes=[bTMP1])
                P.op("dve", (lambda t_=t: (lambda e: e.tensor_tensor(
                    out=y_ap(t_), in0=TMP1[:], in1=PB[:], op=OP.subtract)))(),
                     reads=[bTMP1, bPB], writes=[bY[t]])
            for t in range(NT):
                dstp = yd_r[b_][t][:, c0_:c0_ + C]
                P.dma("sp", (lambda d_=dstp, t_=t: (lambda e: e.dma_start(
                    out=d_, in_=y_ap(t_))))(),
                      reads=[bY[t]], writes=[bYD])

        if g + 1 < NG:
            ensure_ring(g + 1, 0)
            pending_ln4[0] = ln4_finish
        else:
            ln4_finish()

    es = ExitStack()
    esem = {e: es.enter_context(nc.semaphore(f"sem_{e}"))
            for e in ("pe", "dve", "act", "pool")}
    esem["sp"] = None
    dsem = {q: [es.enter_context(nc.semaphore(f"dma_{q}{i}")) for i in range(NDMASEM)]
            for q in ("sp", "pool")}
    block = es.enter_context(nc.Block())
    P.finalize(nc, block, esem, dsem)
    es.close()
    return nc


def _tape_layout(w):
    din, dout = w.shape
    ti, to = din // 128, dout // 128
    return np.ascontiguousarray(
        w.reshape(ti, 128, to, 128).transpose(1, 2, 0, 3).reshape(128, -1))


def _build_tapes(i):
    kl = _tape_layout(i["L0_attn_w_key"])
    vl = _tape_layout(i["L0_attn_w_value"])
    rl = _tape_layout(i["L0_attn_w_recept"])
    kvr = np.stack([kl.reshape(128, 8, 1024), vl.reshape(128, 8, 1024),
                    rl.reshape(128, 8, 1024)], axis=2).reshape(128, -1)
    tape_f = np.concatenate([
        kvr,
        _tape_layout(i["L0_attn_w_output"]),
        _tape_layout(i["L0_attn_f_w_recept"]),
        _tape_layout(i["L0_attn_f_w_key"]),
        _tape_layout(i["L0_attn_f_w_value"]),
    ], axis=1)
    gl = _tape_layout(np.ascontiguousarray(i["L0_ffn_w_gate"].T))
    ul = _tape_layout(np.ascontiguousarray(i["L0_ffn_w_up"].T))
    gu = np.stack([gl.reshape(128, 24, 1024), ul.reshape(128, 24, 1024)],
                  axis=2).reshape(128, -1)
    tape_i = np.concatenate([
        gu,
        _tape_layout(np.ascontiguousarray(i["L0_ffn_w_down"].T)),
    ], axis=1)
    tape_f = np.ascontiguousarray(tape_f, dtype=np.float32)
    tape_i = np.ascontiguousarray(tape_i, dtype=np.int32)
    assert tape_f.shape[1] == FF_COLS and tape_i.shape[1] == TC - FF_COLS
    return tape_f, tape_i


def _build_consts(i):
    def cols(v):
        return np.asarray(v).reshape(8, 128).T
    order = ["L0_attn_ln1_w", "L0_attn_ln1_b", "L0_attn_ln2_w", "L0_attn_ln2_b",
             "L0_attn_time_decay", "L0_attn_time_first", "L0_attn_time_mix_k",
             "L0_attn_time_mix_v", "L0_attn_time_mix_r", "L0_attn_f_time_mix_k",
             "L0_attn_f_time_mix_r"]
    cst = np.ascontiguousarray(
        np.concatenate([cols(i[k]) for k in order], axis=1), dtype=np.float32)
    cstr = np.ascontiguousarray(
        np.concatenate([np.asarray(i["L0_attn_ln1_w"]).ravel(),
                        np.asarray(i["L0_attn_ln2_w"]).ravel()]).reshape(1, -1),
        dtype=np.float32)
    return cst, cstr


def kernel(**inputs):
    x = np.asarray(inputs["L0_attn_x"], dtype=np.float32)
    xt = np.ascontiguousarray(x.transpose(0, 2, 1))
    tape_f, tape_i = _build_tapes(inputs)
    consts, cstr = _build_consts(inputs)

    nc = build_nc()
    in_maps = []
    for c in range(8):
        shard = np.ascontiguousarray(
            xt[c * B_CORE:(c + 1) * B_CORE].reshape(B_CORE * D, T))
        in_maps.append({"xin": shard, "wtf": tape_f, "wti": tape_i,
                        "consts": consts, "cstr": cstr})
    res = run_bass_kernel_spmd(nc, in_maps, core_ids=list(range(8)))
    outs = []
    for c in range(8):
        yc = np.asarray(res.results[c]["y"]).reshape(B_CORE, D, T)
        outs.append(yc.transpose(0, 2, 1))
    return np.ascontiguousarray(np.concatenate(outs, axis=0), dtype=np.float32)


# version 19
# speedup vs baseline: 1.0188x; 1.0188x over previous
import numpy as np
import ml_dtypes
import concourse.bass as bass
import concourse.mybir as mybir
from concourse.bass_utils import run_bass_kernel_spmd
from contextlib import ExitStack

F32 = mybir.dt.float32
BF16 = mybir.dt.bfloat16
I32 = mybir.dt.int32
AF = mybir.ActivationFunctionType
OP = mybir.AluOpType

DBG = []
B_CORE = 4
T = 2048
D = 1024
NT = 8
C = 512
NCH = T // C
PIECE = 4096
NS = 3
PF = 2
EPS = 1e-5

KVR_BASE = 0
WOUT_BASE = 24576
FREC_BASE = 32768
FKEY_BASE = 40960
FVAL_BASE = 73728
FF_COLS = 106496
GU_BASE = 106496
DOWN_BASE = 155648
TC = 180224
NPIECE = TC // PIECE


def col_kvr(w, g, t):  return KVR_BASE + g * 3072 + w * 1024 + t * 128
def col_wout(g, t):    return WOUT_BASE + g * 1024 + t * 128
def col_frec(g, t):    return FREC_BASE + g * 1024 + t * 128
def col_fkey(h, t):    return FKEY_BASE + h * 1024 + t * 128
def col_fval(d, th):   return FVAL_BASE + d * 4096 + th * 128
def col_gu(w, h, t):   return GU_BASE + h * 2048 + w * 1024 + t * 128
def col_down(d, th):   return DOWN_BASE + d * 3072 + th * 128


class Buf:
    __slots__ = ("writer", "readers", "name")
    def __init__(self, name=""):
        self.writer = None
        self.readers = []
        self.name = name


class _Op:
    __slots__ = ("eng", "emit", "deps", "needs_inc", "val", "kind", "slot")


NDMASEM = 6


class Prog:
    ENGS = ("pe", "dve", "act", "pool", "sp")

    def __init__(self):
        self.streams = {e: [] for e in self.ENGS}
        self.dma_count = {e: 0 for e in self.ENGS}

    def _mk(self, eng, emit, kind, deps, reads, writes):
        o = _Op()
        o.eng = eng
        o.emit = emit
        o.kind = kind
        o.needs_inc = False
        o.val = 0
        d = set(deps)
        for b in reads:
            if b.writer is not None:
                d.add(b.writer)
            b.readers.append(o)
        for b in writes:
            if b.writer is not None:
                d.add(b.writer)
            for r in b.readers:
                d.add(r)
        for b in writes:
            b.writer = o
            b.readers = []
        d.discard(o)
        o.deps = list(d)
        self.streams[eng].append(o)
        return o

    def op(self, eng, emit, deps=(), reads=(), writes=()):
        return self._mk(eng, emit, "c", deps, reads, writes)

    def dma(self, queue, emit, deps=(), reads=(), writes=()):
        o = self._mk(queue, emit, "dma", deps, reads, writes)
        i = self.dma_count[queue]
        self.dma_count[queue] += 1
        o.slot = i % NDMASEM
        o.val = 16 * (i // NDMASEM + 1)
        return o

    def finalize(self, nc, block, esem, dsem):
        for e in self.ENGS:
            for o in self.streams[e]:
                for dep in o.deps:
                    if dep.kind == "c" and not (dep.eng == o.eng == "pe"):
                        dep.needs_inc = True
        for e in self.ENGS:
            cnt = 0
            for o in self.streams[e]:
                if o.kind == "c" and o.needs_inc:
                    cnt += 1
                    o.val = cnt
        waits = {}
        for e in self.ENGS:
            water = {}
            for o in self.streams[e]:
                w = {}
                for dep in o.deps:
                    if dep.kind == "dma":
                        key = ("d", dep.eng, dep.slot)
                    elif not (dep.eng == o.eng == "pe"):
                        key = ("e", dep.eng, 0)
                    else:
                        continue
                    if dep.val > w.get(key, 0):
                        w[key] = dep.val
                if o.kind == "dma" and o.val > 16:
                    key = ("d", o.eng, o.slot)
                    if o.val - 16 > w.get(key, 0):
                        w[key] = o.val - 16
                ww = []
                for key, v in w.items():
                    if v > water.get(key, 0):
                        water[key] = v
                        ww.append((key, v))
                waits[id(o)] = ww

        def emit_stream(ename):
            def run(eng):
                for o in self.streams[ename]:
                    for (kind, src, slot), v in waits[id(o)]:
                        sem = dsem[src][slot] if kind == "d" else esem[src]
                        eng.wait_ge(sem, v)
                    inst = o.emit(eng)
                    if o.kind == "dma":
                        inst.then_inc(dsem[o.eng][o.slot], 16)
                    elif o.needs_inc:
                        inst.then_inc(esem[o.eng], 1)
                if ename == "sp":
                    n = self.dma_count["sp"]
                    for s in range(min(n, NDMASEM)):
                        tot = (n - s + NDMASEM - 1) // NDMASEM
                        eng.wait_ge(dsem["sp"][s], 16 * tot)
            return run

        block.tensor(emit_stream("pe"))
        block.vector(emit_stream("dve"))
        block.scalar(emit_stream("act"))
        block.gpsimd(emit_stream("pool"))
        block.sync(emit_stream("sp"))


def build_nc():
    nc = bass.Bass("TRN2", target_bir_lowering=False)

    xd = nc.dram_tensor("xin", [B_CORE * D, T], F32, kind="ExternalInput").ap()
    yd = nc.dram_tensor("y", [B_CORE * D, T], F32, kind="ExternalOutput").ap()
    wtf = nc.dram_tensor("wtf", [128, FF_COLS], F32, kind="ExternalInput").ap()
    wti = nc.dram_tensor("wti", [128, TC - FF_COLS], I32, kind="ExternalInput").ap()
    cst_d = nc.dram_tensor("consts", [128, 88], F32, kind="ExternalInput").ap()
    cstr_d = nc.dram_tensor("cstr", [1, 2 * D], F32, kind="ExternalInput").ap()
    tapes = [nc.dram_tensor(f"wtape{j}", [128, PIECE], BF16, kind="Internal").ap()
             for j in range(NPIECE)]

    A = nc.alloc_sbuf_tensor

    XIN = [A(f"XIN{t}", [128, C], F32) for t in range(NT)]
    XRES = [A(f"XRES{t}", [128, C], F32) for t in range(NT)]
    XN = [A(f"XN{t}", [128, C + 1], BF16) for t in range(NT)]
    Y = [A(f"Y{t}", [128, C], F32) for t in range(NT)]
    RING = [A(f"RING{s}", [128, PIECE], BF16) for s in range(NS)]
    ARENA = [A(f"AR{h}", [128, C], BF16) for h in range(32)]
    MIXK = [A(f"MIXK{t}", [128, C], BF16) for t in range(NT)]
    MIXV = [A(f"MIXV{t}", [128, C], BF16) for t in range(NT)]
    MIXR = [A(f"MIXR{t}", [128, C], BF16) for t in range(NT)]
    SIG = [A(f"SIG{t}", [128, C], F32) for t in range(NT)]
    PT = [A(f"PT{i}", [128, C], F32) for i in range(2)]
    PVT = A("PVT", [128, C], F32)
    AL = A("AL", [128, C + 1], F32)
    BE = A("BE", [128, C + 1], F32)
    NUM = A("NUM", [128, C], F32)
    SC = [A(f"SC{i}", [128, C], F32) for i in range(2)]
    SRT = [A(f"SRT{i}", [128, C], F32) for i in range(2)]
    FTMP = [A(f"FTMP{i}", [128, C], F32) for i in range(2)]
    TMP1 = A("TMP1", [128, C], F32)
    TMPD = A("TMPD", [128, C], BF16)
    XB = [A(f"XB{i}", [128, C], BF16) for i in range(4)]
    XSQ = [A(f"XSQ{i}", [128, C], BF16) for i in range(4)]
    R_MU = A("R_MU", [1, C], F32)
    R_M2 = A("R_M2", [1, C], F32)
    R_MUSQ = A("R_MUSQ", [1, C], F32)
    R_S = A("R_S", [1, C], F32)
    R_B = A("R_B", [1, C], F32)
    R_HIF = A("R_HIF", [1, C], F32)
    R_HI = A("R_HI", [1, C], BF16)
    R_LO = A("R_LO", [1, C], BF16)
    CST = A("CST", [128, 88], F32)
    CDW8 = A("CDW8", [128, 8], F32)
    CDLAM = A("CDLAM", [128, 8], F32)
    CDEU = A("CDEU", [128, 8], F32)
    CDN = A("CDN", [128, 8], F32)
    CSTRB = A("CSTRB", [1, 2 * D], BF16)
    CARA = A("CARA", [128, NT], F32)
    CARB = A("CARB", [128, NT], F32)
    C1 = A("C1", [128, NT], F32)
    C2 = A("C2", [128, NT], F32)
    ONESC = A("ONESC", [128, 1], BF16)
    ONESR = A("ONESR", [1, 128], BF16)

    PS1 = nc.alloc_psum_tensor("PS1", [1, C], F32)
    PS2 = nc.alloc_psum_tensor("PS2", [1, C], F32)
    PA = nc.alloc_psum_tensor("PA", [128, C], F32)
    PB = nc.alloc_psum_tensor("PB", [128, C], F32)
    PM = [nc.alloc_psum_tensor(f"PM{i}", [128, C], F32) for i in range(4)]

    P = Prog()

    bXIN = [Buf(f"xin{t}") for t in range(NT)]
    bXRES = [Buf(f"xres{t}") for t in range(NT)]
    bXN = [Buf(f"xn{t}") for t in range(NT)]
    bARENA = [Buf(f"ar{h}") for h in range(32)]
    bMIXK = [Buf(f"mk{t}") for t in range(NT)]
    bMIXV = [Buf(f"mv{t}") for t in range(NT)]
    bMIXR = [Buf(f"mr{t}") for t in range(NT)]
    bSIG = [Buf(f"sig{t}") for t in range(NT)]
    bPT = [Buf("pt0"), Buf("pt1")]
    bPVT = Buf("pvt")
    bAL = Buf("al")
    bBE = Buf("be")
    bNUM = Buf("num")
    bSC = [Buf("sc0"), Buf("sc1")]
    bSRT = [Buf("srt0"), Buf("srt1")]
    bFTMP = [Buf("ft0"), Buf("ft1")]
    bTMP1 = Buf("tmp1")
    bTMPD = Buf("tmpd")
    bXB = [Buf(f"xb{i}") for i in range(4)]
    bXSQ = [Buf(f"xsq{i}") for i in range(4)]
    bMU, bM2, bMUSQ = Buf("mu"), Buf("m2"), Buf("musq")
    bSROW, bBROW, bHIF = Buf("srow"), Buf("brow"), Buf("hif")
    bSHI, bSLO, bBHI, bBLO = Buf("shi"), Buf("slo"), Buf("bhi"), Buf("blo")
    bCST = Buf("cst")
    bCDW, bCDN, bCDLAM, bCDEU = Buf("cdw"), Buf("cdn"), Buf("cdlam"), Buf("cdeu")
    bCARA, bCARB, bC1, bC2 = Buf("cara"), Buf("carb"), Buf("c1"), Buf("c2")
    bONESC, bONESR = Buf("onesc"), Buf("onesr")
    bCSTRB = Buf("cstrb")
    bY = [Buf(f"yy{t}") for t in range(NT)]
    bPS1, bPS2, bPA, bPB = Buf("ps1"), Buf("ps2"), Buf("pa"), Buf("pb")
    bPM = [Buf(f"pm{i}") for i in range(4)]
    bTAPE = [Buf(f"tape{j}") for j in range(NPIECE)]
    bSLOT = [Buf(f"slot{s}") for s in range(NS)]
    bYD = Buf("yd")

    def xin_ap(t):   return XIN[t][:]
    def xres_ap(t):  return XRES[t][:]
    def y_ap(t):     return Y[t][:]
    def ar_ap(h):    return ARENA[h][:]
    def mixk_ap(t):  return MIXK[t][:]
    def mixv_ap(t):  return MIXV[t][:]
    def mixr_ap(t):  return MIXR[t][:]
    def sig_ap(t):   return SIG[t][:]
    def xn_ap(t):    return XN[t][:, 1:C + 1]
    def lx_ap(t):    return XN[t][:, 0:C]
    def xn0_ap(t):   return XN[t][:, 0:1]
    def xnL_ap(t):   return XN[t][:, C:C + 1]
    def cst_col(base, t): return CST[:, base + t: base + t + 1]
    LN1W, LN1B, LN2W, LN2B, TDC, TFC = 0, 8, 16, 24, 32, 40
    TMK, TMV, TMR, FTMK, FTMR = 48, 56, 64, 72, 80
    def lam_col(t): return CDLAM[:, t: t + 1]
    def eu_col(t):  return CDEU[:, t: t + 1]

    dcst = P.dma("sp", lambda e: e.dma_start(out=CST[:], in_=cst_d), writes=[bCST])
    P.op("act", lambda e: e.activation(out=CDW8[:], in_=CST[:, TDC:TDC + 8],
                                       func=AF.Exp, bias=0.0, scale=1.0),
         reads=[bCST], writes=[bCDW])
    P.op("dve", lambda e: e.tensor_scalar(out=CDN[:], in0=CDW8[:], scalar1=-1.0,
                                          scalar2=None, op0=OP.mult),
         reads=[bCDW], writes=[bCDN])
    P.op("act", lambda e: e.activation(out=CDLAM[:], in_=CDN[:],
                                       func=AF.Exp, bias=0.0, scale=1.0),
         reads=[bCDN], writes=[bCDLAM])
    P.op("act", lambda e: e.activation(out=CDEU[:], in_=CST[:, TFC:TFC + 8],
                                       func=AF.Exp, bias=0.0, scale=1.0),
         reads=[bCST], writes=[bCDEU])
    P.dma("pool", lambda e: e.dma_start(out=CSTRB[:], in_=cstr_d), writes=[bCSTRB])
    P.op("dve", lambda e: e.memset(ONESC[:], 1.0), writes=[bONESC])
    P.op("dve", lambda e: e.memset(ONESR[:], 1.0), writes=[bONESR])

    NPF = FF_COLS // PIECE
    for j in range(NPIECE):
        if j < NPF:
            src = wtf[:, j * PIECE:(j + 1) * PIECE]
        else:
            src = wti[:, (j - NPF) * PIECE:(j - NPF + 1) * PIECE]
        dst = tapes[j][:]
        P.dma("pool", (lambda s=src, d=dst: (lambda e: e.dma_start(out=d, in_=s)))(),
              writes=[bTAPE[j]])


    tapped = set()

    def tap(label, ap_or_parts, bufs=None):
        if label not in DBG or label in tapped:
            return
        tapped.add(label)
        if bufs is not None:
            parts = [(ap_or_parts, list(bufs))]
        else:
            parts = [(a, list(bb)) for a, bb in ap_or_parts]
        tot = sum(a.shape[1] for a, _ in parts)
        td = nc.dram_tensor(f"tap_{label}", [parts[0][0].shape[0], tot],
                            parts[0][0].dtype, kind="ExternalOutput").ap()
        off = 0
        for a, bb in parts:
            d = td[:, off:off + a.shape[1]]
            off += a.shape[1]
            P.dma("sp", (lambda a_=a, d_=d: (lambda e: e.dma_start(out=d_, in_=a_)))(),
                  reads=bb)

    tap("cst", CST[:], [bCST])
    tap("tape0", tapes[0][:], [bTAPE[0]])
    tap("tape26", tapes[26][:], [bTAPE[26]])

    ring_emitted = {}

    def ensure_ring(g, j):
        for jj in range(j, min(j + PF + 1, NPIECE)):
            if (g, jj) in ring_emitted:
                continue
            ring_emitted[(g, jj)] = True
            gp = g * NPIECE + jj
            s = gp % NS
            dst = RING[s][:]
            src = tapes[jj][:]
            P.dma("sp", (lambda s_=src, d_=dst: (lambda e: e.dma_start(out=d_, in_=s_)))(),
                  reads=[bTAPE[jj]], writes=[bSLOT[s]])

    def ring_lhsT(g, col):
        j = col // PIECE
        gp = g * NPIECE + j
        s = gp % NS
        off = col % PIECE
        return RING[s][:, off: off + 128], bSLOT[s]

    def mm(g, ps, ps_buf, col, rhs_ap, rhs_buf, start, stop):
        ensure_ring(g, col // PIECE)
        la, sbuf_ = ring_lhsT(g, col)
        reads = [sbuf_, rhs_buf]
        writes = [ps_buf] if (start or stop) else []
        if not start:
            reads = reads + [ps_buf]
        return P.op("pe",
                    (lambda la_=la, r_=rhs_ap, p_=ps, st=start, sp_=stop:
                     (lambda e: e.matmul(p_[:], la_, r_, start=st, stop=sp_,
                                         skip_group_check=True)))(),
                    reads=reads, writes=writes)

    def ln_stat_cast(t, src_one, src_buf):
        r = t % 4
        P.op("dve", (lambda r_=r, s_=src_one: (lambda e: e.tensor_copy(
            out=XB[r_][:], in_=s_)))(),
             reads=[src_buf], writes=[bXB[r]])
        P.op("act", (lambda r_=r, s_=src_one: (lambda e: e.activation(
            out=XSQ[r_][:], in_=s_, func=AF.Square, bias=0.0, scale=1.0)))(),
             reads=[src_buf], writes=[bXSQ[r]])

    def ln_stat_mm(t):
        r = t % 4
        P.op("pe", (lambda t_=t, r_=r: (lambda e: e.matmul(
            PS1[:], ONESC[:], XB[r_][:], start=(t_ == 0), stop=(t_ == NT - 1),
            skip_group_check=True)))(),
             reads=[bXB[r], bONESC] + ([bPS1] if t else []),
             writes=[bPS1] if (t == 0 or t == NT - 1) else [])
        P.op("pe", (lambda t_=t, r_=r: (lambda e: e.matmul(
            PS2[:], ONESC[:], XSQ[r_][:], start=(t_ == 0), stop=(t_ == NT - 1),
            skip_group_check=True)))(),
             reads=[bXSQ[r], bONESC] + ([bPS2] if t else []),
             writes=[bPS2] if (t == 0 or t == NT - 1) else [])

    def ln_rows():
        MU, M2, MUSQ = R_MU[:], R_M2[:], R_MUSQ[:]
        SROW, BROW = R_S[:], R_B[:]
        P.op("dve", lambda e: e.tensor_scalar(out=MU, in0=PS1[:], scalar1=1.0 / D,
                                              scalar2=None, op0=OP.mult),
             reads=[bPS1], writes=[bMU])
        P.op("act", lambda e: e.activation(out=M2, in_=PS2[:], func=AF.Identity,
                                           bias=0.0, scale=1.0 / D),
             reads=[bPS2], writes=[bM2])
        P.op("dve", lambda e: e.tensor_tensor(out=MUSQ, in0=MU, in1=MU, op=OP.mult),
             reads=[bMU], writes=[bMUSQ])
        P.op("dve", lambda e: e.scalar_tensor_tensor(out=M2, in0=M2, scalar=EPS,
                                                     in1=MUSQ, op0=OP.add,
                                                     op1=OP.subtract),
             reads=[bM2, bMUSQ], writes=[bM2])
        P.op("act", lambda e: e.activation(out=M2, in_=M2, func=AF.Ln,
                                           bias=0.0, scale=1.0),
             reads=[bM2], writes=[bM2])
        P.op("act", lambda e: e.activation(out=SROW, in_=M2, func=AF.Exp,
                                           bias=0.0, scale=-0.5),
             reads=[bM2], writes=[bSROW])
        P.op("dve", lambda e: e.tensor_copy(out=R_HI[:], in_=SROW),
             reads=[bSROW], writes=[bSHI])
        P.op("dve", lambda e: e.tensor_tensor(out=BROW, in0=MU, in1=SROW, op=OP.mult),
             reads=[bMU, bSROW], writes=[bBROW])
        P.op("dve", lambda e: e.tensor_copy(out=R_LO[:], in_=BROW),
             reads=[bBROW], writes=[bSLO])

    def outer_bcast():
        P.op("pe", lambda e: e.matmul(PA[:], ONESR[:], R_HI[:], start=True,
                                      stop=True, skip_group_check=True),
             reads=[bSHI, bONESR], writes=[bPA])
        P.op("pe", lambda e: e.matmul(PB[:], ONESR[:], R_LO[:], start=True,
                                      stop=True, skip_group_check=True),
             reads=[bSLO, bONESR], writes=[bPB])

    PAIR_AB = (((PA, PB), (bPA, bPB)),)
    PAIR_AB23 = (((PA, PB), (bPA, bPB)), ((PM[2], PM[3]), (bPM[2], bPM[3])))
    PAIR_3 = (((PA, PB), (bPA, bPB)), ((PM[2], PM[3]), (bPM[2], bPM[3])),
              ((PM[0], PM[1]), (bPM[0], bPM[1])))

    def apply_tile(t, ci_this, wbase, bbase, src_ap_, src_bufs, carry, carrybuf,
                   pairs):
        if True:
            if ci_this == 0:
                P.op("dve", (lambda t_=t: (lambda e: e.memset(xn0_ap(t_), 0.0)))(),
                     writes=[bXN[t]])
            else:
                P.op("dve", (lambda t_=t, cb=carry: (lambda e: e.tensor_copy(
                    out=xn0_ap(t_), in_=cb[:, t_:t_ + 1])))(),
                     reads=[carrybuf], writes=[bXN[t]])
            P.op("dve", (lambda t_=t: (lambda e: e.tensor_tensor(
                out=TMP1[:], in0=src_ap_(t_), in1=PA[:], op=OP.mult)))(),
                 reads=[src_bufs[t], bPA], writes=[bTMP1])
            P.op("dve", (lambda t_=t: (lambda e: e.tensor_tensor(
                out=TMP1[:], in0=TMP1[:], in1=PB[:], op=OP.subtract)))(),
                 reads=[bTMP1, bPB], writes=[bTMP1])
            P.op("dve", (lambda t_=t, wb=wbase, bb=bbase: (lambda e: e.tensor_scalar(
                out=xn_ap(t_), in0=TMP1[:], scalar1=cst_col(wb, t_),
                scalar2=cst_col(bb, t_), op0=OP.mult, op1=OP.add)))(),
                 reads=[bTMP1, bCST], writes=[bXN[t]])

    def mix_tile(t, mixes):
        P.op("dve", (lambda t_=t: (lambda e: e.tensor_tensor(
            out=TMPD[:], in0=xn_ap(t_), in1=lx_ap(t_), op=OP.subtract)))(),
             reads=[bXN[t]], writes=[bTMPD])
        for base, MAP, mb in mixes:
            P.op("dve", (lambda t_=t, b_=base, m_=MAP: (lambda e: e.scalar_tensor_tensor(
                out=m_(t_), in0=TMPD[:], scalar=cst_col(b_, t_), in1=lx_ap(t_),
                op0=OP.mult, op1=OP.add)))(),
                 reads=[bTMPD, bXN[t], bCST], writes=[mb[t]])

    def carry_saves(carry, carrybuf):
        for t in range(NT):
            P.op("dve", (lambda t_=t, cb=carry: (lambda e: e.tensor_copy(
                out=cb[:, t_:t_ + 1], in_=xnL_ap(t_))))(),
                 reads=[bXN[t]], writes=[carrybuf])

    def inner_apply(ci_this, wbase, bbase, src_ap_, src_bufs, carry, carrybuf,
                    pairs):
        for t in range(NT):
            apply_tile(t, ci_this, wbase, bbase, src_ap_, src_bufs, carry,
                       carrybuf, pairs)

    def inner_mixes(mixes, carry, carrybuf):
        for t in range(NT):
            mix_tile(t, mixes)
        carry_saves(carry, carrybuf)

    xd_r = xd.rearrange("(b t p) c -> b t p c", b=B_CORE, t=NT, p=128)
    yd_r = yd.rearrange("(b t p) c -> b t p c", b=B_CORE, t=NT, p=128)

    def emit_x_load(gq):
        bq, ciq = gq // NCH, gq % NCH
        for t in range(NT):
            srcx = xd_r[bq][t][:, ciq * C: ciq * C + C]
            P.dma("sp", (lambda s_=srcx, t_=t: (lambda e: e.dma_start(
                out=XIN[t_][:], in_=s_)))(),
                  writes=[bXIN[t]])

    LN1_MIXES = ((TMK, mixk_ap, bMIXK), (TMV, mixv_ap, bMIXV),
                 (TMR, mixr_ap, bMIXR))
    LN2_MIXES = ((FTMK, mixv_ap, bMIXV), (FTMR, mixr_ap, bMIXR))

    emit_x_load(0)
    for t in range(NT):
        ln_stat_cast(t, xin_ap(t), bXIN[t])
        ln_stat_mm(t)
    ln_rows()
    outer_bcast()
    inner_apply(0, LN1W, LN1B, xin_ap, bXIN, C1, bC1, PAIR_AB23)
    inner_mixes(LN1_MIXES, C1, bC1)

    NG = B_CORE * NCH
    pending_ln4 = [None]

    for g in range(NG):
        b, ci = g // NCH, g % NCH
        c0 = ci * C

        for t in range(NT):
            kb, bkb = PM[t % 2], bPM[t % 2]
            vb, bvb = (PM[2], bPM[2]) if t % 2 == 0 else (PA, bPA)
            rb, brb = (PM[3], bPM[3]) if t % 2 == 0 else (PB, bPB)
            for kt in range(NT):
                mm(g, kb, bkb, col_kvr(0, t, kt), mixk_ap(kt), bMIXK[kt],
                   kt == 0, kt == NT - 1)
            for kt in range(NT):
                mm(g, vb, bvb, col_kvr(1, t, kt), mixv_ap(kt), bMIXV[kt],
                   kt == 0, kt == NT - 1)
            for kt in range(NT):
                mm(g, rb, brb, col_kvr(2, t, kt), mixr_ap(kt), bMIXR[kt],
                   kt == 0, kt == NT - 1)
            if t == 2 and pending_ln4[0] is not None:
                pending_ln4[0]()
                pending_ln4[0] = None
            r = t % 2
            P.op("act", (lambda r_=r, k_=kb: (lambda e: e.activation(
                out=PT[r_][:], in_=k_[:], func=AF.Exp, bias=0.0, scale=1.0)))(),
                 reads=[bkb], writes=[bPT[r]])
            P.op("dve", (lambda r_=r, v_=vb: (lambda e: e.tensor_tensor(
                out=PVT[:], in0=PT[r_][:], in1=v_[:], op=OP.mult)))(),
                 reads=[bPT[r], bvb], writes=[bPVT])
            P.op("act", (lambda r_=r, rr=rb: (lambda e: e.activation(
                out=SRT[r_][:], in_=rr[:], func=AF.Exp, bias=0.0, scale=-1.0)))(),
                 reads=[brb], writes=[bSRT[r]])
            ia = 0.0 if ci == 0 else CARA[:, t:t + 1]
            ib = 0.0 if ci == 0 else CARB[:, t:t + 1]
            car_reads = [] if ci == 0 else [bCARA]
            cbr_reads = [] if ci == 0 else [bCARB]
            P.op("dve", (lambda t_=t, i_=ia: (lambda e: e.tensor_tensor_scan(
                out=AL[:, 1:C + 1], data0=lam_col(t_).broadcast_to([128, C]),
                data1=PVT[:], initial=i_, op0=OP.mult, op1=OP.add)))(),
                 reads=[bPVT, bCDLAM] + car_reads, writes=[bAL])
            P.op("dve", (lambda t_=t, r_=r, i_=ib: (lambda e: e.tensor_tensor_scan(
                out=BE[:, 1:C + 1], data0=lam_col(t_).broadcast_to([128, C]),
                data1=PT[r_][:], initial=i_, op0=OP.mult, op1=OP.add)))(),
                 reads=[bPT[r], bCDLAM] + cbr_reads, writes=[bBE])
            if ci == 0:
                P.op("dve", (lambda t_=t: (lambda e: e.tensor_scalar(
                    out=NUM[:, 0:1], in0=PVT[:, 0:1], scalar1=eu_col(t_),
                    scalar2=None, op0=OP.mult)))(),
                     reads=[bPVT, bCDEU], writes=[bNUM])
            else:
                P.op("dve", (lambda t_=t: (lambda e: e.scalar_tensor_tensor(
                    out=NUM[:, 0:1], in0=PVT[:, 0:1], scalar=eu_col(t_),
                    in1=CARA[:, t_:t_ + 1], op0=OP.mult, op1=OP.add)))(),
                     reads=[bPVT, bCDEU, bCARA], writes=[bNUM])
            P.op("dve", (lambda t_=t: (lambda e: e.scalar_tensor_tensor(
                out=NUM[:, 1:C], in0=PVT[:, 1:C], scalar=eu_col(t_),
                in1=AL[:, 1:C], op0=OP.mult, op1=OP.add)))(),
                 reads=[bPVT, bCDEU, bAL], writes=[bNUM])
            if ci == 0:
                P.op("dve", (lambda t_=t, r_=r: (lambda e: e.tensor_scalar(
                    out=SC[r_][:, 0:1], in0=PT[r_][:, 0:1], scalar1=eu_col(t_),
                    scalar2=None, op0=OP.mult)))(),
                     reads=[bPT[r], bCDEU], writes=[bSC[r]])
            else:
                P.op("dve", (lambda t_=t, r_=r: (lambda e: e.scalar_tensor_tensor(
                    out=SC[r_][:, 0:1], in0=PT[r_][:, 0:1], scalar=eu_col(t_),
                    in1=CARB[:, t_:t_ + 1], op0=OP.mult, op1=OP.add)))(),
                     reads=[bPT[r], bCDEU, bCARB], writes=[bSC[r]])
            P.op("dve", (lambda t_=t, r_=r: (lambda e: e.scalar_tensor_tensor(
                out=SC[r_][:, 1:C], in0=PT[r_][:, 1:C], scalar=eu_col(t_),
                in1=BE[:, 1:C], op0=OP.mult, op1=OP.add)))(),
                 reads=[bPT[r], bCDEU, bBE], writes=[bSC[r]])
            P.op("dve", (lambda t_=t: (lambda e: e.tensor_copy(
                out=CARA[:, t_:t_ + 1], in_=AL[:, C:C + 1])))(),
                 reads=[bAL], writes=[bCARA])
            P.op("dve", (lambda t_=t: (lambda e: e.tensor_copy(
                out=CARB[:, t_:t_ + 1], in_=BE[:, C:C + 1])))(),
                 reads=[bBE], writes=[bCARB])
            P.op("act", (lambda r_=r: (lambda e: e.activation(
                out=SC[r_][:], in_=SC[r_][:], func=AF.Ln, bias=0.0, scale=1.0)))(),
                 reads=[bSC[r]], writes=[bSC[r]])
            P.op("act", (lambda r_=r: (lambda e: e.activation(
                out=SC[r_][:], in_=SC[r_][:], func=AF.Exp, bias=0.0, scale=-1.0)))(),
                 reads=[bSC[r]], writes=[bSC[r]])
            P.op("act", (lambda r_=r: (lambda e: e.activation(
                out=SRT[r_][:], in_=SRT[r_][:], func=AF.Ln, bias=1.0, scale=1.0)))(),
                 reads=[bSRT[r]], writes=[bSRT[r]])
            P.op("act", (lambda r_=r: (lambda e: e.activation(
                out=SRT[r_][:], in_=SRT[r_][:], func=AF.Exp, bias=0.0, scale=-1.0)))(),
                 reads=[bSRT[r]], writes=[bSRT[r]])
            P.op("dve", (lambda r_=r: (lambda e: e.tensor_tensor(
                out=NUM[:], in0=NUM[:], in1=SC[r_][:], op=OP.mult)))(),
                 reads=[bNUM, bSC[r]], writes=[bNUM])
            P.op("dve", (lambda t_=t, r_=r: (lambda e: e.tensor_tensor(
                out=ar_ap(t_), in0=NUM[:], in1=SRT[r_][:], op=OP.mult)))(),
                 reads=[bNUM, bSRT[r]], writes=[bARENA[t]])

        for gg in range(NT):
            ps, bps = PM[gg % 4], bPM[gg % 4]
            for kt in range(NT):
                mm(g, ps, bps, col_wout(gg, kt), ar_ap(kt), bARENA[kt],
                   kt == 0, kt == NT - 1)
            P.op("dve", (lambda g_=gg, p_=ps: (lambda e: e.tensor_tensor(
                out=xres_ap(g_), in0=xin_ap(g_), in1=p_[:], op=OP.add)))(),
                 reads=[bXIN[gg], bps], writes=[bXRES[gg]])
            ln_stat_cast(gg, xres_ap(gg), bXRES[gg])
            if gg >= 2:
                ln_stat_mm(gg - 2)
        ln_stat_mm(NT - 2)
        ln_stat_mm(NT - 1)

        ln_rows()
        outer_bcast()
        inner_apply(ci, LN2W, LN2B, xres_ap, bXRES, C2, bC2, PAIR_3)
        inner_mixes(LN2_MIXES, C2, bC2)

        for gg in range(NT):
            ps, bps = PM[2 + gg % 2], bPM[2 + gg % 2]
            for kt in range(NT):
                mm(g, ps, bps, col_frec(gg, kt), mixr_ap(kt), bMIXR[kt],
                   kt == 0, kt == NT - 1)
            P.op("act", (lambda g_=gg, p_=ps: (lambda e: e.activation(
                out=sig_ap(g_), in_=p_[:], func=AF.Exp, bias=0.0, scale=-1.0)))(),
                 reads=[bps], writes=[bSIG[gg]])

        for h in range(32):
            ps, bps = PM[h % 4], bPM[h % 4]
            for kt in range(NT):
                mm(g, ps, bps, col_fkey(h, kt), mixv_ap(kt), bMIXV[kt],
                   kt == 0, kt == NT - 1)
            P.op("act", (lambda h_=h, p_=ps: (lambda e: e.activation(
                out=ar_ap(h_), in_=p_[:], func=AF.Relu, bias=0.0, scale=1.0)))(),
                 reads=[bps], writes=[bARENA[h]])
            P.op("dve", (lambda h_=h: (lambda e: e.tensor_tensor(
                out=ar_ap(h_), in0=ar_ap(h_), in1=ar_ap(h_), op=OP.mult)))(),
                 reads=[bARENA[h]], writes=[bARENA[h]])
            if h < 2 * NT and h % 2 == 1:
                gg2 = h // 2
                P.op("act", (lambda g_=gg2: (lambda e: e.activation(
                    out=sig_ap(g_), in_=sig_ap(g_), func=AF.Ln, bias=1.0, scale=1.0)))(),
                     reads=[bSIG[gg2]], writes=[bSIG[gg2]])
                P.op("act", (lambda g_=gg2: (lambda e: e.activation(
                    out=sig_ap(g_), in_=sig_ap(g_), func=AF.Exp, bias=0.0, scale=-1.0)))(),
                     reads=[bSIG[gg2]], writes=[bSIG[gg2]])

        for dd in range(NT):
            ps, bps = PM[dd % 4], bPM[dd % 4]
            for th in range(32):
                mm(g, ps, bps, col_fval(dd, th), ar_ap(th), bARENA[th],
                   th == 0, th == 31)
            P.op("dve", (lambda d_=dd, p_=ps: (lambda e: e.tensor_tensor(
                out=TMP1[:], in0=sig_ap(d_), in1=p_[:], op=OP.mult)))(),
                 reads=[bSIG[dd], bps], writes=[bTMP1])
            P.op("dve", (lambda d_=dd: (lambda e: e.tensor_tensor(
                out=xres_ap(d_), in0=xres_ap(d_), in1=TMP1[:], op=OP.add)))(),
                 reads=[bXRES[dd], bTMP1], writes=[bXRES[dd]])
            P.op("dve", (lambda t_=dd: (lambda e: e.tensor_tensor(
                out=xres_ap(t_), in0=xres_ap(t_), in1=xin_ap(t_), op=OP.add)))(),
                 reads=[bXRES[dd], bXIN[dd]], writes=[bXRES[dd]])
            ln_stat_cast(dd, xres_ap(dd), bXRES[dd])
            if dd >= 2:
                ln_stat_mm(dd - 2)
        ln_stat_mm(NT - 2)
        ln_stat_mm(NT - 1)

        ln_rows()
        outer_bcast()
        for t in range(NT):
            P.op("dve", (lambda t_=t: (lambda e: e.tensor_tensor(
                out=TMP1[:], in0=xres_ap(t_), in1=PA[:], op=OP.mult)))(),
                 reads=[bXRES[t], bPA], writes=[bTMP1])
            P.op("dve", (lambda t_=t: (lambda e: e.tensor_tensor(
                out=y_ap(t_), in0=TMP1[:], in1=PB[:], op=OP.subtract)))(),
                 reads=[bTMP1, bPB], writes=[bY[t]])
            P.op("dve", (lambda t_=t: (lambda e: e.tensor_copy(
                out=ar_ap(24 + t_), in_=y_ap(t_))))(),
                 reads=[bY[t]], writes=[bARENA[24 + t]])

        if g + 1 < NG:
            emit_x_load(g + 1)

        for h in range(24):
            gp, bgp = PM[h % 2], bPM[h % 2]
            up, bup = PM[2 + h % 2], bPM[2 + h % 2]
            for kt in range(NT):
                mm(g, gp, bgp, col_gu(0, h, kt), ar_ap(24 + kt), bARENA[24 + kt],
                   kt == 0, kt == NT - 1)
            for kt in range(NT):
                mm(g, up, bup, col_gu(1, h, kt), ar_ap(24 + kt), bARENA[24 + kt],
                   kt == 0, kt == NT - 1)
            r = h % 2
            P.op("dve", (lambda r_=r, p_=gp: (lambda e: e.tensor_scalar_max(
                out=FTMP[r_][:], in0=p_[:], scalar1=-5000.0)))(),
                 reads=[bgp], writes=[bFTMP[r]])
            P.op("act", (lambda r_=r: (lambda e: e.activation(
                out=FTMP[r_][:], in_=FTMP[r_][:], func=AF.Exp, bias=0.0,
                scale=-1.0 / 127.0)))(),
                 reads=[bFTMP[r]], writes=[bFTMP[r]])
            P.op("act", (lambda r_=r: (lambda e: e.activation(
                out=FTMP[r_][:], in_=FTMP[r_][:], func=AF.Ln, bias=1.0, scale=1.0)))(),
                 reads=[bFTMP[r]], writes=[bFTMP[r]])
            P.op("act", (lambda r_=r: (lambda e: e.activation(
                out=FTMP[r_][:], in_=FTMP[r_][:], func=AF.Exp, bias=0.0, scale=-1.0)))(),
                 reads=[bFTMP[r]], writes=[bFTMP[r]])
            P.op("dve", (lambda r_=r, p_=gp: (lambda e: e.tensor_tensor(
                out=FTMP[r_][:], in0=FTMP[r_][:], in1=p_[:], op=OP.mult)))(),
                 reads=[bFTMP[r], bgp], writes=[bFTMP[r]])
            P.op("dve", (lambda r_=r, h_=h, p_=up: (lambda e: e.scalar_tensor_tensor(
                out=ar_ap(h_), in0=FTMP[r_][:], scalar=1.0 / (127.0 * 127.0),
                in1=p_[:], op0=OP.mult, op1=OP.mult)))(),
                 reads=[bFTMP[r], bup], writes=[bARENA[h]])
            if g + 1 < NG and 8 <= h < 16:
                ln_stat_cast(h - 8, xin_ap(h - 8), bXIN[h - 8])
            if g + 1 < NG and 11 <= h < 19:
                ln_stat_mm(h - 11)
        if g + 1 < NG:
            ln_rows()
            outer_bcast()

        for dd in range(NT):
            ps, bps = PM[dd % 4], bPM[dd % 4]
            for th in range(24):
                mm(g, ps, bps, col_down(dd, th), ar_ap(th), bARENA[th],
                   th == 0, th == 23)
            P.op("dve", (lambda d_=dd, p_=ps: (lambda e: e.scalar_tensor_tensor(
                out=xres_ap(d_), in0=p_[:], scalar=1.0 / 127.0, in1=y_ap(d_),
                op0=OP.mult, op1=OP.add)))(),
                 reads=[bY[dd], bps], writes=[bXRES[dd]])
            ln_stat_cast(dd, xres_ap(dd), bXRES[dd])
            if dd >= 2:
                ln_stat_mm(dd - 2)
            if g + 1 < NG:
                apply_tile(dd, (g + 1) % NCH, LN1W, LN1B, xin_ap, bXIN, C1, bC1,
                           PAIR_AB)
                mix_tile(dd, LN1_MIXES)
        ln_stat_mm(NT - 2)
        ln_stat_mm(NT - 1)
        if g + 1 < NG:
            carry_saves(C1, bC1)
        ln_rows()

        def ln4_finish(b_=b, c0_=c0):
            outer_bcast()
            for t in range(NT):
                P.op("dve", (lambda t_=t: (lambda e: e.tensor_tensor(
                    out=TMP1[:], in0=xres_ap(t_), in1=PA[:], op=OP.mult)))(),
                     reads=[bXRES[t], bPA], writes=[bTMP1])
                P.op("dve", (lambda t_=t: (lambda e: e.tensor_tensor(
                    out=y_ap(t_), in0=TMP1[:], in1=PB[:], op=OP.subtract)))(),
                     reads=[bTMP1, bPB], writes=[bY[t]])
            for t in range(NT):
                dstp = yd_r[b_][t][:, c0_:c0_ + C]
                P.dma("sp", (lambda d_=dstp, t_=t: (lambda e: e.dma_start(
                    out=d_, in_=y_ap(t_))))(),
                      reads=[bY[t]], writes=[bYD])

        if g + 1 < NG:
            ensure_ring(g + 1, 0)
            pending_ln4[0] = ln4_finish
        else:
            ln4_finish()

    es = ExitStack()
    esem = {e: es.enter_context(nc.semaphore(f"sem_{e}"))
            for e in ("pe", "dve", "act", "pool")}
    esem["sp"] = None
    dsem = {q: [es.enter_context(nc.semaphore(f"dma_{q}{i}")) for i in range(NDMASEM)]
            for q in ("sp", "pool")}
    block = es.enter_context(nc.Block())
    P.finalize(nc, block, esem, dsem)
    es.close()
    return nc


def _tape_layout(w):
    din, dout = w.shape
    ti, to = din // 128, dout // 128
    return np.ascontiguousarray(
        w.reshape(ti, 128, to, 128).transpose(1, 2, 0, 3).reshape(128, -1))


def _build_tapes(i):
    kl = _tape_layout(i["L0_attn_w_key"])
    vl = _tape_layout(i["L0_attn_w_value"])
    rl = _tape_layout(i["L0_attn_w_recept"])
    kvr = np.stack([kl.reshape(128, 8, 1024), vl.reshape(128, 8, 1024),
                    rl.reshape(128, 8, 1024)], axis=2).reshape(128, -1)
    tape_f = np.concatenate([
        kvr,
        _tape_layout(i["L0_attn_w_output"]),
        _tape_layout(i["L0_attn_f_w_recept"]),
        _tape_layout(i["L0_attn_f_w_key"]),
        _tape_layout(i["L0_attn_f_w_value"]),
    ], axis=1)
    gl = _tape_layout(np.ascontiguousarray(i["L0_ffn_w_gate"].T))
    ul = _tape_layout(np.ascontiguousarray(i["L0_ffn_w_up"].T))
    gu = np.stack([gl.reshape(128, 24, 1024), ul.reshape(128, 24, 1024)],
                  axis=2).reshape(128, -1)
    tape_i = np.concatenate([
        gu,
        _tape_layout(np.ascontiguousarray(i["L0_ffn_w_down"].T)),
    ], axis=1)
    tape_f = np.ascontiguousarray(tape_f, dtype=np.float32)
    tape_i = np.ascontiguousarray(tape_i, dtype=np.int32)
    assert tape_f.shape[1] == FF_COLS and tape_i.shape[1] == TC - FF_COLS
    return tape_f, tape_i


def _build_consts(i):
    def cols(v):
        return np.asarray(v).reshape(8, 128).T
    order = ["L0_attn_ln1_w", "L0_attn_ln1_b", "L0_attn_ln2_w", "L0_attn_ln2_b",
             "L0_attn_time_decay", "L0_attn_time_first", "L0_attn_time_mix_k",
             "L0_attn_time_mix_v", "L0_attn_time_mix_r", "L0_attn_f_time_mix_k",
             "L0_attn_f_time_mix_r"]
    cst = np.ascontiguousarray(
        np.concatenate([cols(i[k]) for k in order], axis=1), dtype=np.float32)
    cstr = np.ascontiguousarray(
        np.concatenate([np.asarray(i["L0_attn_ln1_w"]).ravel(),
                        np.asarray(i["L0_attn_ln2_w"]).ravel()]).reshape(1, -1),
        dtype=np.float32)
    return cst, cstr


def kernel(**inputs):
    x = np.asarray(inputs["L0_attn_x"], dtype=np.float32)
    xt = np.ascontiguousarray(x.transpose(0, 2, 1))
    tape_f, tape_i = _build_tapes(inputs)
    consts, cstr = _build_consts(inputs)

    nc = build_nc()
    in_maps = []
    for c in range(8):
        shard = np.ascontiguousarray(
            xt[c * B_CORE:(c + 1) * B_CORE].reshape(B_CORE * D, T))
        in_maps.append({"xin": shard, "wtf": tape_f, "wti": tape_i,
                        "consts": consts, "cstr": cstr})
    res = run_bass_kernel_spmd(nc, in_maps, core_ids=list(range(8)))
    outs = []
    for c in range(8):
        yc = np.asarray(res.results[c]["y"]).reshape(B_CORE, D, T)
        outs.append(yc.transpose(0, 2, 1))
    return np.ascontiguousarray(np.concatenate(outs, axis=0), dtype=np.float32)


# version 20
# speedup vs baseline: 1.1270x; 1.1062x over previous
import numpy as np
import ml_dtypes
import concourse.bass as bass
import concourse.mybir as mybir
from concourse.bass_utils import run_bass_kernel_spmd
from contextlib import ExitStack

F32 = mybir.dt.float32
BF16 = mybir.dt.bfloat16
I32 = mybir.dt.int32
AF = mybir.ActivationFunctionType
OP = mybir.AluOpType

DBG = []
B_CORE = 4
T = 2048
D = 1024
NT = 8
C = 512
NCH = T // C
PIECE = 4096
NS = 3
PF = 2
EPS = 1e-5

KVR_BASE = 0
WOUT_BASE = 24576
FREC_BASE = 32768
FKEY_BASE = 40960
FVAL_BASE = 73728
FF_COLS = 106496
GU_BASE = 106496
DOWN_BASE = 155648
TC = 180224
NPIECE = TC // PIECE


def col_kvr(w, g, t):  return KVR_BASE + g * 3072 + w * 1024 + t * 128
def col_wout(g, t):    return WOUT_BASE + g * 1024 + t * 128
def col_frec(g, t):    return FREC_BASE + g * 1024 + t * 128
def col_fkey(h, t):    return FKEY_BASE + h * 1024 + t * 128
def col_fval(d, th):   return FVAL_BASE + d * 4096 + th * 128
def col_gu(w, h, t):   return GU_BASE + h * 2048 + w * 1024 + t * 128
def col_down(d, th):   return DOWN_BASE + d * 3072 + th * 128


class Buf:
    __slots__ = ("writer", "readers", "name")
    def __init__(self, name=""):
        self.writer = None
        self.readers = []
        self.name = name


class _Op:
    __slots__ = ("eng", "emit", "deps", "needs_inc", "val", "kind", "slot", "big")


NDMASEM = 6


class Prog:
    ENGS = ("pe", "dve", "act", "pool", "sp")

    def __init__(self):
        self.streams = {e: [] for e in self.ENGS}
        self.dma_count = {e: 0 for e in self.ENGS}

    def _mk(self, eng, emit, kind, deps, reads, writes, big=False):
        o = _Op()
        o.eng = eng
        o.emit = emit
        o.kind = kind
        o.needs_inc = False
        o.big = big
        o.val = 0
        d = set(deps)
        for b in reads:
            if b.writer is not None:
                d.add(b.writer)
            b.readers.append(o)
        for b in writes:
            if b.writer is not None:
                d.add(b.writer)
            for r in b.readers:
                d.add(r)
        for b in writes:
            b.writer = o
            b.readers = []
        d.discard(o)
        o.deps = list(d)
        self.streams[eng].append(o)
        return o

    def op(self, eng, emit, deps=(), reads=(), writes=(), big=False):
        return self._mk(eng, emit, "c", deps, reads, writes, big)

    def dma(self, queue, emit, deps=(), reads=(), writes=()):
        o = self._mk(queue, emit, "dma", deps, reads, writes)
        i = self.dma_count[queue]
        self.dma_count[queue] += 1
        o.slot = i % NDMASEM
        o.val = 16 * (i // NDMASEM + 1)
        return o

    def finalize(self, nc, block, esem, dsem):
        for e in self.ENGS:
            for o in self.streams[e]:
                for dep in o.deps:
                    if dep.kind == "c" and not (dep.eng == o.eng == "pe") and \
                            not (dep.eng == o.eng and dep.big):
                        dep.needs_inc = True
        for e in self.ENGS:
            cnt = 0
            for o in self.streams[e]:
                if o.kind == "c" and o.needs_inc:
                    cnt += 1
                    o.val = cnt
        waits = {}
        for e in self.ENGS:
            water = {}
            for o in self.streams[e]:
                w = {}
                for dep in o.deps:
                    if dep.kind == "dma":
                        key = ("d", dep.eng, dep.slot)
                    elif dep.eng != o.eng:
                        key = ("e", dep.eng, 0)
                    elif o.eng == "pe" or dep.big:
                        continue
                    else:
                        key = ("e", dep.eng, 0)
                    if dep.val > w.get(key, 0):
                        w[key] = dep.val
                if o.kind == "dma" and o.val > 16:
                    key = ("d", o.eng, o.slot)
                    if o.val - 16 > w.get(key, 0):
                        w[key] = o.val - 16
                ww = []
                for key, v in w.items():
                    if v > water.get(key, 0):
                        water[key] = v
                        ww.append((key, v))
                waits[id(o)] = ww

        def emit_stream(ename):
            def run(eng):
                for o in self.streams[ename]:
                    for (kind, src, slot), v in waits[id(o)]:
                        sem = dsem[src][slot] if kind == "d" else esem[src]
                        eng.wait_ge(sem, v)
                    inst = o.emit(eng)
                    if o.kind == "dma":
                        inst.then_inc(dsem[o.eng][o.slot], 16)
                    elif o.needs_inc:
                        inst.then_inc(esem[o.eng], 1)
                if ename == "sp":
                    n = self.dma_count["sp"]
                    for s in range(min(n, NDMASEM)):
                        tot = (n - s + NDMASEM - 1) // NDMASEM
                        eng.wait_ge(dsem["sp"][s], 16 * tot)
            return run

        block.tensor(emit_stream("pe"))
        block.vector(emit_stream("dve"))
        block.scalar(emit_stream("act"))
        block.gpsimd(emit_stream("pool"))
        block.sync(emit_stream("sp"))


def build_nc():
    nc = bass.Bass("TRN2", target_bir_lowering=False)

    xd = nc.dram_tensor("xin", [B_CORE * D, T], F32, kind="ExternalInput").ap()
    yd = nc.dram_tensor("y", [B_CORE * D, T], F32, kind="ExternalOutput").ap()
    wtf = nc.dram_tensor("wtf", [128, FF_COLS], F32, kind="ExternalInput").ap()
    wti = nc.dram_tensor("wti", [128, TC - FF_COLS], I32, kind="ExternalInput").ap()
    cst_d = nc.dram_tensor("consts", [128, 88], F32, kind="ExternalInput").ap()
    cstr_d = nc.dram_tensor("cstr", [1, 2 * D], F32, kind="ExternalInput").ap()
    tapes = [nc.dram_tensor(f"wtape{j}", [128, PIECE], BF16, kind="Internal").ap()
             for j in range(NPIECE)]

    A = nc.alloc_sbuf_tensor

    XIN = [A(f"XIN{t}", [128, C], F32) for t in range(NT)]
    XRES = [A(f"XRES{t}", [128, C], F32) for t in range(NT)]
    XN = [A(f"XN{t}", [128, C + 1], BF16) for t in range(NT)]
    Y = [A(f"Y{t}", [128, C], F32) for t in range(NT)]
    RING = [A(f"RING{s}", [128, PIECE], BF16) for s in range(NS)]
    ARENA = [A(f"AR{h}", [128, C], BF16) for h in range(32)]
    MIXK = [A(f"MIXK{t}", [128, C], BF16) for t in range(NT)]
    MIXV = [A(f"MIXV{t}", [128, C], BF16) for t in range(NT)]
    MIXR = [A(f"MIXR{t}", [128, C], BF16) for t in range(NT)]
    SIG = [A(f"SIG{t}", [128, C], F32) for t in range(NT)]
    PT = [A(f"PT{i}", [128, C], F32) for i in range(2)]
    PVT = A("PVT", [128, C], F32)
    AL = A("AL", [128, C + 1], F32)
    BE = A("BE", [128, C + 1], F32)
    NUM = A("NUM", [128, C], F32)
    SC = [A(f"SC{i}", [128, C], F32) for i in range(2)]
    SRT = [A(f"SRT{i}", [128, C], F32) for i in range(2)]
    FTMP = [A(f"FTMP{i}", [128, C], F32) for i in range(2)]
    TMP1 = A("TMP1", [128, C], F32)
    TMPD = A("TMPD", [128, C], BF16)
    XB = [A(f"XB{i}", [128, C], BF16) for i in range(4)]
    XSQ = [A(f"XSQ{i}", [128, C], BF16) for i in range(4)]
    R_MU = A("R_MU", [1, C], F32)
    R_M2 = A("R_M2", [1, C], F32)
    R_MUSQ = A("R_MUSQ", [1, C], F32)
    R_S = A("R_S", [1, C], F32)
    R_B = A("R_B", [1, C], F32)
    R_HIF = A("R_HIF", [1, C], F32)
    R_HI = A("R_HI", [1, C], BF16)
    R_LO = A("R_LO", [1, C], BF16)
    CST = A("CST", [128, 88], F32)
    CDW8 = A("CDW8", [128, 8], F32)
    CDLAM = A("CDLAM", [128, 8], F32)
    CDEU = A("CDEU", [128, 8], F32)
    CDN = A("CDN", [128, 8], F32)
    CSTRB = A("CSTRB", [1, 2 * D], BF16)
    CARA = A("CARA", [128, NT], F32)
    CARB = A("CARB", [128, NT], F32)
    C1 = A("C1", [128, NT], F32)
    C2 = A("C2", [128, NT], F32)
    ONESC = A("ONESC", [128, 1], BF16)
    ONESR = A("ONESR", [1, 128], BF16)

    PS1 = nc.alloc_psum_tensor("PS1", [1, C], F32)
    PS2 = nc.alloc_psum_tensor("PS2", [1, C], F32)
    PA = nc.alloc_psum_tensor("PA", [128, C], F32)
    PB = nc.alloc_psum_tensor("PB", [128, C], F32)
    PM = [nc.alloc_psum_tensor(f"PM{i}", [128, C], F32) for i in range(4)]

    P = Prog()

    bXIN = [Buf(f"xin{t}") for t in range(NT)]
    bXRES = [Buf(f"xres{t}") for t in range(NT)]
    bXN = [Buf(f"xn{t}") for t in range(NT)]
    bARENA = [Buf(f"ar{h}") for h in range(32)]
    bMIXK = [Buf(f"mk{t}") for t in range(NT)]
    bMIXV = [Buf(f"mv{t}") for t in range(NT)]
    bMIXR = [Buf(f"mr{t}") for t in range(NT)]
    bSIG = [Buf(f"sig{t}") for t in range(NT)]
    bPT = [Buf("pt0"), Buf("pt1")]
    bPVT = Buf("pvt")
    bAL = Buf("al")
    bBE = Buf("be")
    bNUM = Buf("num")
    bSC = [Buf("sc0"), Buf("sc1")]
    bSRT = [Buf("srt0"), Buf("srt1")]
    bFTMP = [Buf("ft0"), Buf("ft1")]
    bTMP1 = Buf("tmp1")
    bTMPD = Buf("tmpd")
    bXB = [Buf(f"xb{i}") for i in range(4)]
    bXSQ = [Buf(f"xsq{i}") for i in range(4)]
    bMU, bM2, bMUSQ = Buf("mu"), Buf("m2"), Buf("musq")
    bSROW, bBROW, bHIF = Buf("srow"), Buf("brow"), Buf("hif")
    bSHI, bSLO, bBHI, bBLO = Buf("shi"), Buf("slo"), Buf("bhi"), Buf("blo")
    bCST = Buf("cst")
    bCDW, bCDN, bCDLAM, bCDEU = Buf("cdw"), Buf("cdn"), Buf("cdlam"), Buf("cdeu")
    bCARA, bCARB, bC1, bC2 = Buf("cara"), Buf("carb"), Buf("c1"), Buf("c2")
    bONESC, bONESR = Buf("onesc"), Buf("onesr")
    bCSTRB = Buf("cstrb")
    bY = [Buf(f"yy{t}") for t in range(NT)]
    bPS1, bPS2, bPA, bPB = Buf("ps1"), Buf("ps2"), Buf("pa"), Buf("pb")
    bPM = [Buf(f"pm{i}") for i in range(4)]
    bTAPE = [Buf(f"tape{j}") for j in range(NPIECE)]
    bSLOT = [Buf(f"slot{s}") for s in range(NS)]
    bYD = Buf("yd")

    def xin_ap(t):   return XIN[t][:]
    def xres_ap(t):  return XRES[t][:]
    def y_ap(t):     return Y[t][:]
    def ar_ap(h):    return ARENA[h][:]
    def mixk_ap(t):  return MIXK[t][:]
    def mixv_ap(t):  return MIXV[t][:]
    def mixr_ap(t):  return MIXR[t][:]
    def sig_ap(t):   return SIG[t][:]
    def xn_ap(t):    return XN[t][:, 1:C + 1]
    def lx_ap(t):    return XN[t][:, 0:C]
    def xn0_ap(t):   return XN[t][:, 0:1]
    def xnL_ap(t):   return XN[t][:, C:C + 1]
    def cst_col(base, t): return CST[:, base + t: base + t + 1]
    LN1W, LN1B, LN2W, LN2B, TDC, TFC = 0, 8, 16, 24, 32, 40
    TMK, TMV, TMR, FTMK, FTMR = 48, 56, 64, 72, 80
    def lam_col(t): return CDLAM[:, t: t + 1]
    def eu_col(t):  return CDEU[:, t: t + 1]

    dcst = P.dma("sp", lambda e: e.dma_start(out=CST[:], in_=cst_d), writes=[bCST])
    P.op("act", lambda e: e.activation(out=CDW8[:], in_=CST[:, TDC:TDC + 8],
                                       func=AF.Exp, bias=0.0, scale=1.0),
         reads=[bCST], writes=[bCDW])
    P.op("dve", lambda e: e.tensor_scalar(out=CDN[:], in0=CDW8[:], scalar1=-1.0,
                                          scalar2=None, op0=OP.mult),
         reads=[bCDW], writes=[bCDN])
    P.op("act", lambda e: e.activation(out=CDLAM[:], in_=CDN[:],
                                       func=AF.Exp, bias=0.0, scale=1.0),
         reads=[bCDN], writes=[bCDLAM])
    P.op("act", lambda e: e.activation(out=CDEU[:], in_=CST[:, TFC:TFC + 8],
                                       func=AF.Exp, bias=0.0, scale=1.0),
         reads=[bCST], writes=[bCDEU])
    P.dma("pool", lambda e: e.dma_start(out=CSTRB[:], in_=cstr_d), writes=[bCSTRB])
    P.op("dve", lambda e: e.memset(ONESC[:], 1.0), writes=[bONESC])
    P.op("dve", lambda e: e.memset(ONESR[:], 1.0), writes=[bONESR])

    NPF = FF_COLS // PIECE
    for j in range(NPIECE):
        if j < NPF:
            src = wtf[:, j * PIECE:(j + 1) * PIECE]
        else:
            src = wti[:, (j - NPF) * PIECE:(j - NPF + 1) * PIECE]
        dst = tapes[j][:]
        P.dma("pool", (lambda s=src, d=dst: (lambda e: e.dma_start(out=d, in_=s)))(),
              writes=[bTAPE[j]])


    tapped = set()

    def tap(label, ap_or_parts, bufs=None):
        if label not in DBG or label in tapped:
            return
        tapped.add(label)
        if bufs is not None:
            parts = [(ap_or_parts, list(bufs))]
        else:
            parts = [(a, list(bb)) for a, bb in ap_or_parts]
        tot = sum(a.shape[1] for a, _ in parts)
        td = nc.dram_tensor(f"tap_{label}", [parts[0][0].shape[0], tot],
                            parts[0][0].dtype, kind="ExternalOutput").ap()
        off = 0
        for a, bb in parts:
            d = td[:, off:off + a.shape[1]]
            off += a.shape[1]
            P.dma("sp", (lambda a_=a, d_=d: (lambda e: e.dma_start(out=d_, in_=a_)))(),
                  reads=bb)

    tap("cst", CST[:], [bCST])
    tap("tape0", tapes[0][:], [bTAPE[0]])
    tap("tape26", tapes[26][:], [bTAPE[26]])

    ring_emitted = {}

    def ensure_ring(g, j):
        for jj in range(j, min(j + PF + 1, NPIECE)):
            if (g, jj) in ring_emitted:
                continue
            ring_emitted[(g, jj)] = True
            gp = g * NPIECE + jj
            s = gp % NS
            dst = RING[s][:]
            src = tapes[jj][:]
            P.dma("sp", (lambda s_=src, d_=dst: (lambda e: e.dma_start(out=d_, in_=s_)))(),
                  reads=[bTAPE[jj]], writes=[bSLOT[s]])

    def ring_lhsT(g, col):
        j = col // PIECE
        gp = g * NPIECE + j
        s = gp % NS
        off = col % PIECE
        return RING[s][:, off: off + 128], bSLOT[s]

    def mm(g, ps, ps_buf, col, rhs_ap, rhs_buf, start, stop):
        ensure_ring(g, col // PIECE)
        la, sbuf_ = ring_lhsT(g, col)
        reads = [sbuf_, rhs_buf]
        writes = [ps_buf] if (start or stop) else []
        if not start:
            reads = reads + [ps_buf]
        return P.op("pe",
                    (lambda la_=la, r_=rhs_ap, p_=ps, st=start, sp_=stop:
                     (lambda e: e.matmul(p_[:], la_, r_, start=st, stop=sp_,
                                         skip_group_check=True)))(),
                    reads=reads, writes=writes)

    def ln_stat_cast(t, src_one, src_buf):
        r = t % 4
        P.op("dve", (lambda r_=r, s_=src_one: (lambda e: e.tensor_copy(
            out=XB[r_][:], in_=s_)))(),
             reads=[src_buf], writes=[bXB[r]], big=True)
        P.op("act", (lambda r_=r, s_=src_one: (lambda e: e.activation(
            out=XSQ[r_][:], in_=s_, func=AF.Square, bias=0.0, scale=1.0)))(),
             reads=[src_buf], writes=[bXSQ[r]], big=True)

    def ln_stat_mm(t):
        r = t % 4
        P.op("pe", (lambda t_=t, r_=r: (lambda e: e.matmul(
            PS1[:], ONESC[:], XB[r_][:], start=(t_ == 0), stop=(t_ == NT - 1),
            skip_group_check=True)))(),
             reads=[bXB[r], bONESC] + ([bPS1] if t else []),
             writes=[bPS1] if (t == 0 or t == NT - 1) else [])
        P.op("pe", (lambda t_=t, r_=r: (lambda e: e.matmul(
            PS2[:], ONESC[:], XSQ[r_][:], start=(t_ == 0), stop=(t_ == NT - 1),
            skip_group_check=True)))(),
             reads=[bXSQ[r], bONESC] + ([bPS2] if t else []),
             writes=[bPS2] if (t == 0 or t == NT - 1) else [])

    def ln_rows():
        MU, M2, MUSQ = R_MU[:], R_M2[:], R_MUSQ[:]
        SROW, BROW = R_S[:], R_B[:]
        P.op("dve", lambda e: e.tensor_scalar(out=MU, in0=PS1[:], scalar1=1.0 / D,
                                              scalar2=None, op0=OP.mult),
             reads=[bPS1], writes=[bMU])
        P.op("act", lambda e: e.activation(out=M2, in_=PS2[:], func=AF.Identity,
                                           bias=0.0, scale=1.0 / D),
             reads=[bPS2], writes=[bM2])
        P.op("dve", lambda e: e.tensor_tensor(out=MUSQ, in0=MU, in1=MU, op=OP.mult),
             reads=[bMU], writes=[bMUSQ])
        P.op("dve", lambda e: e.scalar_tensor_tensor(out=M2, in0=M2, scalar=EPS,
                                                     in1=MUSQ, op0=OP.add,
                                                     op1=OP.subtract),
             reads=[bM2, bMUSQ], writes=[bM2])
        P.op("act", lambda e: e.activation(out=M2, in_=M2, func=AF.Ln,
                                           bias=0.0, scale=1.0),
             reads=[bM2], writes=[bM2])
        P.op("act", lambda e: e.activation(out=SROW, in_=M2, func=AF.Exp,
                                           bias=0.0, scale=-0.5),
             reads=[bM2], writes=[bSROW])
        P.op("dve", lambda e: e.tensor_copy(out=R_HI[:], in_=SROW),
             reads=[bSROW], writes=[bSHI])
        P.op("dve", lambda e: e.tensor_tensor(out=BROW, in0=MU, in1=SROW, op=OP.mult),
             reads=[bMU, bSROW], writes=[bBROW])
        P.op("dve", lambda e: e.tensor_copy(out=R_LO[:], in_=BROW),
             reads=[bBROW], writes=[bSLO])

    def outer_bcast():
        P.op("pe", lambda e: e.matmul(PA[:], ONESR[:], R_HI[:], start=True,
                                      stop=True, skip_group_check=True),
             reads=[bSHI, bONESR], writes=[bPA])
        P.op("pe", lambda e: e.matmul(PB[:], ONESR[:], R_LO[:], start=True,
                                      stop=True, skip_group_check=True),
             reads=[bSLO, bONESR], writes=[bPB])

    PAIR_AB = (((PA, PB), (bPA, bPB)),)
    PAIR_AB23 = (((PA, PB), (bPA, bPB)), ((PM[2], PM[3]), (bPM[2], bPM[3])))
    PAIR_3 = (((PA, PB), (bPA, bPB)), ((PM[2], PM[3]), (bPM[2], bPM[3])),
              ((PM[0], PM[1]), (bPM[0], bPM[1])))

    def apply_tile(t, ci_this, wbase, bbase, src_ap_, src_bufs, carry, carrybuf,
                   pairs):
        pa_t, pb_t = pairs[t % len(pairs)][0]
        bpa_t, bpb_t = pairs[t % len(pairs)][1]
        if True:
            wrow = CSTRB[0:1, wbase + t * 128: wbase + (t + 1) * 128]
            P.op("pe", (lambda w_=wrow, p_=pa_t: (lambda e: e.matmul(
                p_[:], w_, R_HI[:], start=True, stop=True,
                skip_group_check=True)))(),
                 reads=[bSHI, bCSTRB], writes=[bpa_t])
            P.op("pe", (lambda w_=wrow, p_=pb_t: (lambda e: e.matmul(
                p_[:], w_, R_LO[:], start=True, stop=True,
                skip_group_check=True)))(),
                 reads=[bSLO, bCSTRB], writes=[bpb_t])
            if ci_this == 0:
                P.op("dve", (lambda t_=t: (lambda e: e.memset(xn0_ap(t_), 0.0)))(),
                     writes=[bXN[t]])
            else:
                P.op("dve", (lambda t_=t, cb=carry: (lambda e: e.tensor_copy(
                    out=xn0_ap(t_), in_=cb[:, t_:t_ + 1])))(),
                     reads=[carrybuf], writes=[bXN[t]])
            P.op("dve", (lambda t_=t, p_=pa_t, bp=bpa_t: (lambda e: e.tensor_tensor(
                out=TMP1[:], in0=src_ap_(t_), in1=p_[:], op=OP.mult)))(),
                 reads=[src_bufs[t], bpa_t], writes=[bTMP1], big=True)
            P.op("dve", (lambda t_=t, p_=pb_t, bb=bbase: (lambda e: e.scalar_tensor_tensor(
                out=xn_ap(t_), in0=TMP1[:], scalar=cst_col(bb, t_), in1=p_[:],
                op0=OP.add, op1=OP.subtract)))(),
                 reads=[bTMP1, bpb_t, bCST], writes=[bXN[t]], big=True)

    def mix_tile(t, mixes):
        P.op("dve", (lambda t_=t: (lambda e: e.tensor_tensor(
            out=TMPD[:], in0=xn_ap(t_), in1=lx_ap(t_), op=OP.subtract)))(),
             reads=[bXN[t]], writes=[bTMPD], big=True)
        for base, MAP, mb in mixes:
            P.op("dve", (lambda t_=t, b_=base, m_=MAP: (lambda e: e.scalar_tensor_tensor(
                out=m_(t_), in0=TMPD[:], scalar=cst_col(b_, t_), in1=lx_ap(t_),
                op0=OP.mult, op1=OP.add)))(),
                 reads=[bTMPD, bXN[t], bCST], writes=[mb[t]], big=True)

    def carry_saves(carry, carrybuf):
        for t in range(NT):
            P.op("dve", (lambda t_=t, cb=carry: (lambda e: e.tensor_copy(
                out=cb[:, t_:t_ + 1], in_=xnL_ap(t_))))(),
                 reads=[bXN[t]], writes=[carrybuf])

    def inner_apply(ci_this, wbase, bbase, src_ap_, src_bufs, carry, carrybuf,
                    pairs):
        for t in range(NT):
            apply_tile(t, ci_this, wbase, bbase, src_ap_, src_bufs, carry,
                       carrybuf, pairs)

    def inner_mixes(mixes, carry, carrybuf):
        for t in range(NT):
            mix_tile(t, mixes)
        carry_saves(carry, carrybuf)

    xd_r = xd.rearrange("(b t p) c -> b t p c", b=B_CORE, t=NT, p=128)
    yd_r = yd.rearrange("(b t p) c -> b t p c", b=B_CORE, t=NT, p=128)

    def emit_x_load(gq):
        bq, ciq = gq // NCH, gq % NCH
        for t in range(NT):
            srcx = xd_r[bq][t][:, ciq * C: ciq * C + C]
            P.dma("sp", (lambda s_=srcx, t_=t: (lambda e: e.dma_start(
                out=XIN[t_][:], in_=s_)))(),
                  writes=[bXIN[t]])

    LN1_MIXES = ((TMK, mixk_ap, bMIXK), (TMV, mixv_ap, bMIXV),
                 (TMR, mixr_ap, bMIXR))
    LN2_MIXES = ((FTMK, mixv_ap, bMIXV), (FTMR, mixr_ap, bMIXR))

    emit_x_load(0)
    for t in range(NT):
        ln_stat_cast(t, xin_ap(t), bXIN[t])
        ln_stat_mm(t)
    ln_rows()
    inner_apply(0, 0, LN1B, xin_ap, bXIN, C1, bC1, PAIR_AB23)
    inner_mixes(LN1_MIXES, C1, bC1)

    NG = B_CORE * NCH
    pending_ln4 = [None]

    for g in range(NG):
        b, ci = g // NCH, g % NCH
        c0 = ci * C

        for t in range(NT):
            kb, bkb = PM[t % 2], bPM[t % 2]
            vb, bvb = (PM[2], bPM[2]) if t % 2 == 0 else (PA, bPA)
            rb, brb = (PM[3], bPM[3]) if t % 2 == 0 else (PB, bPB)
            for kt in range(NT):
                mm(g, kb, bkb, col_kvr(0, t, kt), mixk_ap(kt), bMIXK[kt],
                   kt == 0, kt == NT - 1)
            for kt in range(NT):
                mm(g, vb, bvb, col_kvr(1, t, kt), mixv_ap(kt), bMIXV[kt],
                   kt == 0, kt == NT - 1)
            for kt in range(NT):
                mm(g, rb, brb, col_kvr(2, t, kt), mixr_ap(kt), bMIXR[kt],
                   kt == 0, kt == NT - 1)
            if t == 2 and pending_ln4[0] is not None:
                pending_ln4[0]()
                pending_ln4[0] = None
            r = t % 2
            P.op("act", (lambda r_=r, k_=kb: (lambda e: e.activation(
                out=PT[r_][:], in_=k_[:], func=AF.Exp, bias=0.0, scale=1.0)))(),
                 reads=[bkb], writes=[bPT[r]], big=True)
            P.op("dve", (lambda r_=r, v_=vb: (lambda e: e.tensor_tensor(
                out=PVT[:], in0=PT[r_][:], in1=v_[:], op=OP.mult)))(),
                 reads=[bPT[r], bvb], writes=[bPVT], big=True)
            P.op("act", (lambda r_=r, rr=rb: (lambda e: e.activation(
                out=SRT[r_][:], in_=rr[:], func=AF.Exp, bias=0.0, scale=-1.0)))(),
                 reads=[brb], writes=[bSRT[r]], big=True)
            ia = 0.0 if ci == 0 else CARA[:, t:t + 1]
            ib = 0.0 if ci == 0 else CARB[:, t:t + 1]
            car_reads = [] if ci == 0 else [bCARA]
            cbr_reads = [] if ci == 0 else [bCARB]
            P.op("dve", (lambda t_=t, i_=ia: (lambda e: e.tensor_tensor_scan(
                out=AL[:, 1:C + 1], data0=lam_col(t_).broadcast_to([128, C]),
                data1=PVT[:], initial=i_, op0=OP.mult, op1=OP.add)))(),
                 reads=[bPVT, bCDLAM] + car_reads, writes=[bAL], big=True)
            P.op("dve", (lambda t_=t, r_=r, i_=ib: (lambda e: e.tensor_tensor_scan(
                out=BE[:, 1:C + 1], data0=lam_col(t_).broadcast_to([128, C]),
                data1=PT[r_][:], initial=i_, op0=OP.mult, op1=OP.add)))(),
                 reads=[bPT[r], bCDLAM] + cbr_reads, writes=[bBE], big=True)
            if ci == 0:
                P.op("dve", (lambda t_=t: (lambda e: e.tensor_scalar(
                    out=NUM[:, 0:1], in0=PVT[:, 0:1], scalar1=eu_col(t_),
                    scalar2=None, op0=OP.mult)))(),
                     reads=[bPVT, bCDEU], writes=[bNUM])
            else:
                P.op("dve", (lambda t_=t: (lambda e: e.scalar_tensor_tensor(
                    out=NUM[:, 0:1], in0=PVT[:, 0:1], scalar=eu_col(t_),
                    in1=CARA[:, t_:t_ + 1], op0=OP.mult, op1=OP.add)))(),
                     reads=[bPVT, bCDEU, bCARA], writes=[bNUM])
            P.op("dve", (lambda t_=t: (lambda e: e.scalar_tensor_tensor(
                out=NUM[:, 1:C], in0=PVT[:, 1:C], scalar=eu_col(t_),
                in1=AL[:, 1:C], op0=OP.mult, op1=OP.add)))(),
                 reads=[bPVT, bCDEU, bAL], writes=[bNUM], big=True)
            if ci == 0:
                P.op("dve", (lambda t_=t, r_=r: (lambda e: e.tensor_scalar(
                    out=SC[r_][:, 0:1], in0=PT[r_][:, 0:1], scalar1=eu_col(t_),
                    scalar2=None, op0=OP.mult)))(),
                     reads=[bPT[r], bCDEU], writes=[bSC[r]])
            else:
                P.op("dve", (lambda t_=t, r_=r: (lambda e: e.scalar_tensor_tensor(
                    out=SC[r_][:, 0:1], in0=PT[r_][:, 0:1], scalar=eu_col(t_),
                    in1=CARB[:, t_:t_ + 1], op0=OP.mult, op1=OP.add)))(),
                     reads=[bPT[r], bCDEU, bCARB], writes=[bSC[r]])
            P.op("dve", (lambda t_=t, r_=r: (lambda e: e.scalar_tensor_tensor(
                out=SC[r_][:, 1:C], in0=PT[r_][:, 1:C], scalar=eu_col(t_),
                in1=BE[:, 1:C], op0=OP.mult, op1=OP.add)))(),
                 reads=[bPT[r], bCDEU, bBE], writes=[bSC[r]], big=True)
            P.op("dve", (lambda t_=t: (lambda e: e.tensor_copy(
                out=CARA[:, t_:t_ + 1], in_=AL[:, C:C + 1])))(),
                 reads=[bAL], writes=[bCARA])
            P.op("dve", (lambda t_=t: (lambda e: e.tensor_copy(
                out=CARB[:, t_:t_ + 1], in_=BE[:, C:C + 1])))(),
                 reads=[bBE], writes=[bCARB])
            P.op("act", (lambda r_=r: (lambda e: e.activation(
                out=SC[r_][:], in_=SC[r_][:], func=AF.Ln, bias=0.0, scale=1.0)))(),
                 reads=[bSC[r]], writes=[bSC[r]], big=True)
            P.op("act", (lambda r_=r: (lambda e: e.activation(
                out=SC[r_][:], in_=SC[r_][:], func=AF.Exp, bias=0.0, scale=-1.0)))(),
                 reads=[bSC[r]], writes=[bSC[r]], big=True)
            P.op("act", (lambda r_=r: (lambda e: e.activation(
                out=SRT[r_][:], in_=SRT[r_][:], func=AF.Ln, bias=1.0, scale=1.0)))(),
                 reads=[bSRT[r]], writes=[bSRT[r]], big=True)
            P.op("act", (lambda r_=r: (lambda e: e.activation(
                out=SRT[r_][:], in_=SRT[r_][:], func=AF.Exp, bias=0.0, scale=-1.0)))(),
                 reads=[bSRT[r]], writes=[bSRT[r]], big=True)
            P.op("dve", (lambda r_=r: (lambda e: e.tensor_tensor(
                out=NUM[:], in0=NUM[:], in1=SC[r_][:], op=OP.mult)))(),
                 reads=[bNUM, bSC[r]], writes=[bNUM], big=True)
            P.op("dve", (lambda t_=t, r_=r: (lambda e: e.tensor_tensor(
                out=ar_ap(t_), in0=NUM[:], in1=SRT[r_][:], op=OP.mult)))(),
                 reads=[bNUM, bSRT[r]], writes=[bARENA[t]], big=True)

        for gg in range(NT):
            ps, bps = PM[gg % 4], bPM[gg % 4]
            for kt in range(NT):
                mm(g, ps, bps, col_wout(gg, kt), ar_ap(kt), bARENA[kt],
                   kt == 0, kt == NT - 1)
            P.op("dve", (lambda g_=gg, p_=ps: (lambda e: e.tensor_tensor(
                out=xres_ap(g_), in0=xin_ap(g_), in1=p_[:], op=OP.add)))(),
                 reads=[bXIN[gg], bps], writes=[bXRES[gg]], big=True)
            ln_stat_cast(gg, xres_ap(gg), bXRES[gg])
            if gg >= 2:
                ln_stat_mm(gg - 2)
        ln_stat_mm(NT - 2)
        ln_stat_mm(NT - 1)

        ln_rows()
        inner_apply(ci, D, LN2B, xres_ap, bXRES, C2, bC2, PAIR_3)
        inner_mixes(LN2_MIXES, C2, bC2)

        for gg in range(NT):
            ps, bps = PM[2 + gg % 2], bPM[2 + gg % 2]
            for kt in range(NT):
                mm(g, ps, bps, col_frec(gg, kt), mixr_ap(kt), bMIXR[kt],
                   kt == 0, kt == NT - 1)
            P.op("act", (lambda g_=gg, p_=ps: (lambda e: e.activation(
                out=sig_ap(g_), in_=p_[:], func=AF.Exp, bias=0.0, scale=-1.0)))(),
                 reads=[bps], writes=[bSIG[gg]], big=True)

        for h in range(32):
            ps, bps = PM[h % 4], bPM[h % 4]
            for kt in range(NT):
                mm(g, ps, bps, col_fkey(h, kt), mixv_ap(kt), bMIXV[kt],
                   kt == 0, kt == NT - 1)
            P.op("act", (lambda h_=h, p_=ps: (lambda e: e.activation(
                out=ar_ap(h_), in_=p_[:], func=AF.Relu, bias=0.0, scale=1.0)))(),
                 reads=[bps], writes=[bARENA[h]], big=True)
            P.op("dve", (lambda h_=h: (lambda e: e.tensor_tensor(
                out=ar_ap(h_), in0=ar_ap(h_), in1=ar_ap(h_), op=OP.mult)))(),
                 reads=[bARENA[h]], writes=[bARENA[h]], big=True)
            if h < 2 * NT and h % 2 == 1:
                gg2 = h // 2
                P.op("act", (lambda g_=gg2: (lambda e: e.activation(
                    out=sig_ap(g_), in_=sig_ap(g_), func=AF.Ln, bias=1.0, scale=1.0)))(),
                     reads=[bSIG[gg2]], writes=[bSIG[gg2]], big=True)
                P.op("act", (lambda g_=gg2: (lambda e: e.activation(
                    out=sig_ap(g_), in_=sig_ap(g_), func=AF.Exp, bias=0.0, scale=-1.0)))(),
                     reads=[bSIG[gg2]], writes=[bSIG[gg2]], big=True)

        for dd in range(NT):
            ps, bps = PM[dd % 4], bPM[dd % 4]
            for th in range(32):
                mm(g, ps, bps, col_fval(dd, th), ar_ap(th), bARENA[th],
                   th == 0, th == 31)
            P.op("dve", (lambda d_=dd, p_=ps: (lambda e: e.tensor_tensor(
                out=TMP1[:], in0=sig_ap(d_), in1=p_[:], op=OP.mult)))(),
                 reads=[bSIG[dd], bps], writes=[bTMP1], big=True)
            P.op("dve", (lambda d_=dd: (lambda e: e.tensor_tensor(
                out=xres_ap(d_), in0=xres_ap(d_), in1=TMP1[:], op=OP.add)))(),
                 reads=[bXRES[dd], bTMP1], writes=[bXRES[dd]], big=True)
            P.op("dve", (lambda t_=dd: (lambda e: e.tensor_tensor(
                out=xres_ap(t_), in0=xres_ap(t_), in1=xin_ap(t_), op=OP.add)))(),
                 reads=[bXRES[dd], bXIN[dd]], writes=[bXRES[dd]], big=True)
            ln_stat_cast(dd, xres_ap(dd), bXRES[dd])
            if dd >= 2:
                ln_stat_mm(dd - 2)
        ln_stat_mm(NT - 2)
        ln_stat_mm(NT - 1)

        ln_rows()
        outer_bcast()
        for t in range(NT):
            P.op("dve", (lambda t_=t: (lambda e: e.tensor_tensor(
                out=TMP1[:], in0=xres_ap(t_), in1=PA[:], op=OP.mult)))(),
                 reads=[bXRES[t], bPA], writes=[bTMP1], big=True)
            P.op("dve", (lambda t_=t: (lambda e: e.tensor_tensor(
                out=y_ap(t_), in0=TMP1[:], in1=PB[:], op=OP.subtract)))(),
                 reads=[bTMP1, bPB], writes=[bY[t]], big=True)
            P.op("dve", (lambda t_=t: (lambda e: e.tensor_copy(
                out=ar_ap(24 + t_), in_=y_ap(t_))))(),
                 reads=[bY[t]], writes=[bARENA[24 + t]], big=True)

        if g + 1 < NG:
            emit_x_load(g + 1)

        for h in range(24):
            gp, bgp = PM[h % 2], bPM[h % 2]
            up, bup = PM[2 + h % 2], bPM[2 + h % 2]
            for kt in range(NT):
                mm(g, gp, bgp, col_gu(0, h, kt), ar_ap(24 + kt), bARENA[24 + kt],
                   kt == 0, kt == NT - 1)
            for kt in range(NT):
                mm(g, up, bup, col_gu(1, h, kt), ar_ap(24 + kt), bARENA[24 + kt],
                   kt == 0, kt == NT - 1)
            r = h % 2
            P.op("dve", (lambda r_=r, p_=gp: (lambda e: e.tensor_scalar_max(
                out=FTMP[r_][:], in0=p_[:], scalar1=-5000.0)))(),
                 reads=[bgp], writes=[bFTMP[r]], big=True)
            P.op("act", (lambda r_=r: (lambda e: e.activation(
                out=FTMP[r_][:], in_=FTMP[r_][:], func=AF.Exp, bias=0.0,
                scale=-1.0 / 127.0)))(),
                 reads=[bFTMP[r]], writes=[bFTMP[r]], big=True)
            P.op("act", (lambda r_=r: (lambda e: e.activation(
                out=FTMP[r_][:], in_=FTMP[r_][:], func=AF.Ln, bias=1.0, scale=1.0)))(),
                 reads=[bFTMP[r]], writes=[bFTMP[r]], big=True)
            P.op("act", (lambda r_=r: (lambda e: e.activation(
                out=FTMP[r_][:], in_=FTMP[r_][:], func=AF.Exp, bias=0.0, scale=-1.0)))(),
                 reads=[bFTMP[r]], writes=[bFTMP[r]], big=True)
            P.op("dve", (lambda r_=r, p_=gp: (lambda e: e.tensor_tensor(
                out=FTMP[r_][:], in0=FTMP[r_][:], in1=p_[:], op=OP.mult)))(),
                 reads=[bFTMP[r], bgp], writes=[bFTMP[r]], big=True)
            P.op("dve", (lambda r_=r, h_=h, p_=up: (lambda e: e.scalar_tensor_tensor(
                out=ar_ap(h_), in0=FTMP[r_][:], scalar=1.0 / (127.0 * 127.0),
                in1=p_[:], op0=OP.mult, op1=OP.mult)))(),
                 reads=[bFTMP[r], bup], writes=[bARENA[h]], big=True)
            if g + 1 < NG and 8 <= h < 16:
                ln_stat_cast(h - 8, xin_ap(h - 8), bXIN[h - 8])
            if g + 1 < NG and 11 <= h < 19:
                ln_stat_mm(h - 11)
        if g + 1 < NG:
            ln_rows()

        for dd in range(NT):
            ps, bps = PM[dd % 4], bPM[dd % 4]
            for th in range(24):
                mm(g, ps, bps, col_down(dd, th), ar_ap(th), bARENA[th],
                   th == 0, th == 23)
            P.op("dve", (lambda d_=dd, p_=ps: (lambda e: e.scalar_tensor_tensor(
                out=xres_ap(d_), in0=p_[:], scalar=1.0 / 127.0, in1=y_ap(d_),
                op0=OP.mult, op1=OP.add)))(),
                 reads=[bY[dd], bps], writes=[bXRES[dd]], big=True)
            ln_stat_cast(dd, xres_ap(dd), bXRES[dd])
            if dd >= 2:
                ln_stat_mm(dd - 2)
            if g + 1 < NG:
                apply_tile(dd, (g + 1) % NCH, 0, LN1B, xin_ap, bXIN, C1, bC1,
                           PAIR_AB)
                mix_tile(dd, LN1_MIXES)
        ln_stat_mm(NT - 2)
        ln_stat_mm(NT - 1)
        if g + 1 < NG:
            carry_saves(C1, bC1)
        ln_rows()

        def ln4_finish(b_=b, c0_=c0):
            outer_bcast()
            for t in range(NT):
                P.op("dve", (lambda t_=t: (lambda e: e.tensor_tensor(
                    out=TMP1[:], in0=xres_ap(t_), in1=PA[:], op=OP.mult)))(),
                     reads=[bXRES[t], bPA], writes=[bTMP1], big=True)
                P.op("dve", (lambda t_=t: (lambda e: e.tensor_tensor(
                    out=y_ap(t_), in0=TMP1[:], in1=PB[:], op=OP.subtract)))(),
                     reads=[bTMP1, bPB], writes=[bY[t]], big=True)
            for t in range(NT):
                dstp = yd_r[b_][t][:, c0_:c0_ + C]
                P.dma("sp", (lambda d_=dstp, t_=t: (lambda e: e.dma_start(
                    out=d_, in_=y_ap(t_))))(),
                      reads=[bY[t]], writes=[bYD])

        if g + 1 < NG:
            ensure_ring(g + 1, 0)
            pending_ln4[0] = ln4_finish
        else:
            ln4_finish()

    es = ExitStack()
    esem = {e: es.enter_context(nc.semaphore(f"sem_{e}"))
            for e in ("pe", "dve", "act", "pool")}
    esem["sp"] = None
    dsem = {q: [es.enter_context(nc.semaphore(f"dma_{q}{i}")) for i in range(NDMASEM)]
            for q in ("sp", "pool")}
    block = es.enter_context(nc.Block())
    P.finalize(nc, block, esem, dsem)
    es.close()
    return nc


def _tape_layout(w):
    din, dout = w.shape
    ti, to = din // 128, dout // 128
    return np.ascontiguousarray(
        w.reshape(ti, 128, to, 128).transpose(1, 2, 0, 3).reshape(128, -1))


def _build_tapes(i):
    kl = _tape_layout(i["L0_attn_w_key"])
    vl = _tape_layout(i["L0_attn_w_value"])
    rl = _tape_layout(i["L0_attn_w_recept"])
    kvr = np.stack([kl.reshape(128, 8, 1024), vl.reshape(128, 8, 1024),
                    rl.reshape(128, 8, 1024)], axis=2).reshape(128, -1)
    tape_f = np.concatenate([
        kvr,
        _tape_layout(i["L0_attn_w_output"]),
        _tape_layout(i["L0_attn_f_w_recept"]),
        _tape_layout(i["L0_attn_f_w_key"]),
        _tape_layout(i["L0_attn_f_w_value"]),
    ], axis=1)
    gl = _tape_layout(np.ascontiguousarray(i["L0_ffn_w_gate"].T))
    ul = _tape_layout(np.ascontiguousarray(i["L0_ffn_w_up"].T))
    gu = np.stack([gl.reshape(128, 24, 1024), ul.reshape(128, 24, 1024)],
                  axis=2).reshape(128, -1)
    tape_i = np.concatenate([
        gu,
        _tape_layout(np.ascontiguousarray(i["L0_ffn_w_down"].T)),
    ], axis=1)
    tape_f = np.ascontiguousarray(tape_f, dtype=np.float32)
    tape_i = np.ascontiguousarray(tape_i, dtype=np.int32)
    assert tape_f.shape[1] == FF_COLS and tape_i.shape[1] == TC - FF_COLS
    return tape_f, tape_i


def _build_consts(i):
    def cols(v):
        return np.asarray(v).reshape(8, 128).T
    order = ["L0_attn_ln1_w", "L0_attn_ln1_b", "L0_attn_ln2_w", "L0_attn_ln2_b",
             "L0_attn_time_decay", "L0_attn_time_first", "L0_attn_time_mix_k",
             "L0_attn_time_mix_v", "L0_attn_time_mix_r", "L0_attn_f_time_mix_k",
             "L0_attn_f_time_mix_r"]
    cst = np.ascontiguousarray(
        np.concatenate([cols(i[k]) for k in order], axis=1), dtype=np.float32)
    cstr = np.ascontiguousarray(
        np.concatenate([np.asarray(i["L0_attn_ln1_w"]).ravel(),
                        np.asarray(i["L0_attn_ln2_w"]).ravel()]).reshape(1, -1),
        dtype=np.float32)
    return cst, cstr


def kernel(**inputs):
    x = np.asarray(inputs["L0_attn_x"], dtype=np.float32)
    xt = np.ascontiguousarray(x.transpose(0, 2, 1))
    tape_f, tape_i = _build_tapes(inputs)
    consts, cstr = _build_consts(inputs)

    nc = build_nc()
    in_maps = []
    for c in range(8):
        shard = np.ascontiguousarray(
            xt[c * B_CORE:(c + 1) * B_CORE].reshape(B_CORE * D, T))
        in_maps.append({"xin": shard, "wtf": tape_f, "wti": tape_i,
                        "consts": consts, "cstr": cstr})
    res = run_bass_kernel_spmd(nc, in_maps, core_ids=list(range(8)))
    outs = []
    for c in range(8):
        yc = np.asarray(res.results[c]["y"]).reshape(B_CORE, D, T)
        outs.append(yc.transpose(0, 2, 1))
    return np.ascontiguousarray(np.concatenate(outs, axis=0), dtype=np.float32)


# version 21
# speedup vs baseline: 1.1510x; 1.0213x over previous
import numpy as np
import ml_dtypes
import concourse.bass as bass
import concourse.mybir as mybir
from concourse.bass_utils import run_bass_kernel_spmd
from contextlib import ExitStack

F32 = mybir.dt.float32
BF16 = mybir.dt.bfloat16
I32 = mybir.dt.int32
AF = mybir.ActivationFunctionType
OP = mybir.AluOpType

DBG = []
B_CORE = 4
T = 2048
D = 1024
NT = 8
C = 512
NCH = T // C
PIECE = 4096
NS = 3
PF = 2
EPS = 1e-5

KVR_BASE = 0
WOUT_BASE = 24576
FREC_BASE = 32768
FKEY_BASE = 40960
FVAL_BASE = 73728
FF_COLS = 106496
GU_BASE = 106496
DOWN_BASE = 155648
TC = 180224
NPIECE = TC // PIECE


def col_kvr(w, g, t):  return KVR_BASE + g * 3072 + w * 1024 + t * 128
def col_wout(g, t):    return WOUT_BASE + g * 1024 + t * 128
def col_frec(g, t):    return FREC_BASE + g * 1024 + t * 128
def col_fkey(h, t):    return FKEY_BASE + h * 1024 + t * 128
def col_fval(d, th):   return FVAL_BASE + d * 4096 + th * 128
def col_gu(w, h, t):   return GU_BASE + h * 2048 + w * 1024 + t * 128
def col_down(d, th):   return DOWN_BASE + d * 3072 + th * 128


class Buf:
    __slots__ = ("writer", "readers", "name")
    def __init__(self, name=""):
        self.writer = None
        self.readers = []
        self.name = name


class _Op:
    __slots__ = ("eng", "emit", "deps", "needs_inc", "val", "kind", "slot", "big")


NDMASEM = 6


class Prog:
    ENGS = ("pe", "dve", "act", "pool", "sp")

    def __init__(self):
        self.streams = {e: [] for e in self.ENGS}
        self.dma_count = {e: 0 for e in self.ENGS}

    def _mk(self, eng, emit, kind, deps, reads, writes, big=False):
        o = _Op()
        o.eng = eng
        o.emit = emit
        o.kind = kind
        o.needs_inc = False
        o.big = big
        o.val = 0
        d = set(deps)
        for b in reads:
            if b.writer is not None:
                d.add(b.writer)
            b.readers.append(o)
        for b in writes:
            if b.writer is not None:
                d.add(b.writer)
            for r in b.readers:
                d.add(r)
        for b in writes:
            b.writer = o
            b.readers = []
        d.discard(o)
        o.deps = list(d)
        self.streams[eng].append(o)
        return o

    def op(self, eng, emit, deps=(), reads=(), writes=(), big=False):
        return self._mk(eng, emit, "c", deps, reads, writes, big)

    def dma(self, queue, emit, deps=(), reads=(), writes=()):
        o = self._mk(queue, emit, "dma", deps, reads, writes)
        i = self.dma_count[queue]
        self.dma_count[queue] += 1
        o.slot = i % NDMASEM
        o.val = 16 * (i // NDMASEM + 1)
        return o

    def finalize(self, nc, block, esem, dsem):
        for e in self.ENGS:
            for o in self.streams[e]:
                for dep in o.deps:
                    if dep.kind == "c" and not (dep.eng == o.eng == "pe") and \
                            not (dep.eng == o.eng and dep.big):
                        dep.needs_inc = True
        for e in self.ENGS:
            cnt = 0
            for o in self.streams[e]:
                if o.kind == "c" and o.needs_inc:
                    cnt += 1
                    o.val = cnt
        waits = {}
        for e in self.ENGS:
            water = {}
            for o in self.streams[e]:
                w = {}
                for dep in o.deps:
                    if dep.kind == "dma":
                        key = ("d", dep.eng, dep.slot)
                    elif dep.eng != o.eng:
                        key = ("e", dep.eng, 0)
                    elif o.eng == "pe" or dep.big:
                        continue
                    else:
                        key = ("e", dep.eng, 0)
                    if dep.val > w.get(key, 0):
                        w[key] = dep.val
                if o.kind == "dma" and o.val > 16:
                    key = ("d", o.eng, o.slot)
                    if o.val - 16 > w.get(key, 0):
                        w[key] = o.val - 16
                ww = []
                for key, v in w.items():
                    if v > water.get(key, 0):
                        water[key] = v
                        ww.append((key, v))
                waits[id(o)] = ww

        def emit_stream(ename):
            def run(eng):
                for o in self.streams[ename]:
                    for (kind, src, slot), v in waits[id(o)]:
                        sem = dsem[src][slot] if kind == "d" else esem[src]
                        eng.wait_ge(sem, v)
                    inst = o.emit(eng)
                    if o.kind == "dma":
                        inst.then_inc(dsem[o.eng][o.slot], 16)
                    elif o.needs_inc:
                        inst.then_inc(esem[o.eng], 1)
                if ename == "sp":
                    n = self.dma_count["sp"]
                    for s in range(min(n, NDMASEM)):
                        tot = (n - s + NDMASEM - 1) // NDMASEM
                        eng.wait_ge(dsem["sp"][s], 16 * tot)
            return run

        block.tensor(emit_stream("pe"))
        block.vector(emit_stream("dve"))
        block.scalar(emit_stream("act"))
        block.gpsimd(emit_stream("pool"))
        block.sync(emit_stream("sp"))


def build_nc():
    nc = bass.Bass("TRN2", target_bir_lowering=False)

    xd = nc.dram_tensor("xin", [B_CORE * D, T], F32, kind="ExternalInput").ap()
    yd = nc.dram_tensor("y", [B_CORE * D, T], F32, kind="ExternalOutput").ap()
    wtf = nc.dram_tensor("wtf", [128, FF_COLS], F32, kind="ExternalInput").ap()
    wti = nc.dram_tensor("wti", [128, TC - FF_COLS], I32, kind="ExternalInput").ap()
    cst_d = nc.dram_tensor("consts", [128, 88], F32, kind="ExternalInput").ap()
    cstr_d = nc.dram_tensor("cstr", [1, 2 * D], F32, kind="ExternalInput").ap()
    tapes = [nc.dram_tensor(f"wtape{j}", [128, PIECE], BF16, kind="Internal").ap()
             for j in range(NPIECE)]

    A = nc.alloc_sbuf_tensor

    XIN = [A(f"XIN{t}", [128, C], F32) for t in range(NT)]
    XRES = [A(f"XRES{t}", [128, C], F32) for t in range(NT)]
    XN = [A(f"XN{t}", [128, C + 1], BF16) for t in range(NT)]
    Y = [A(f"Y{t}", [128, C], F32) for t in range(NT)]
    RING = [A(f"RING{s}", [128, PIECE], BF16) for s in range(NS)]
    ARENA = [A(f"AR{h}", [128, C], BF16) for h in range(32)]
    MIXK = [A(f"MIXK{t}", [128, C], BF16) for t in range(NT)]
    MIXV = [A(f"MIXV{t}", [128, C], BF16) for t in range(NT)]
    MIXR = [A(f"MIXR{t}", [128, C], BF16) for t in range(NT)]
    SIG = [A(f"SIG{t}", [128, C], F32) for t in range(NT)]
    PT = [A(f"PT{i}", [128, C], F32) for i in range(2)]
    PVT = A("PVT", [128, C], F32)
    AL = A("AL", [128, C + 1], F32)
    BE = A("BE", [128, C + 1], F32)
    NUM = A("NUM", [128, C], F32)
    SC = [A(f"SC{i}", [128, C], F32) for i in range(2)]
    SRT = [A(f"SRT{i}", [128, C], F32) for i in range(2)]
    FTMP = [A(f"FTMP{i}", [128, C], F32) for i in range(2)]
    TMP1 = A("TMP1", [128, C], F32)
    TMPD = A("TMPD", [128, C], BF16)
    XB = [A(f"XB{i}", [128, C], BF16) for i in range(4)]
    XSQ = [A(f"XSQ{i}", [128, C], BF16) for i in range(4)]
    R_MU = A("R_MU", [1, C], F32)
    R_M2 = A("R_M2", [1, C], F32)
    R_MUSQ = A("R_MUSQ", [1, C], F32)
    R_S = A("R_S", [1, C], F32)
    R_B = A("R_B", [1, C], F32)
    R_HIF = A("R_HIF", [1, C], F32)
    R_HI = A("R_HI", [1, C], BF16)
    R_LO = A("R_LO", [1, C], BF16)
    CST = A("CST", [128, 88], F32)
    CDW8 = A("CDW8", [128, 8], F32)
    CDLAM = A("CDLAM", [128, 8], F32)
    CDEU = A("CDEU", [128, 8], F32)
    CDN = A("CDN", [128, 8], F32)
    CSTRB = A("CSTRB", [1, 2 * D], BF16)
    CARA = A("CARA", [128, NT], F32)
    CARB = A("CARB", [128, NT], F32)
    C1 = A("C1", [128, NT], F32)
    C2 = A("C2", [128, NT], F32)
    ONESC = A("ONESC", [128, 1], BF16)
    ONESR = A("ONESR", [1, 128], BF16)

    PS1 = nc.alloc_psum_tensor("PS1", [1, C], F32)
    PS2 = nc.alloc_psum_tensor("PS2", [1, C], F32)
    PA = nc.alloc_psum_tensor("PA", [128, C], F32)
    PB = nc.alloc_psum_tensor("PB", [128, C], F32)
    PM = [nc.alloc_psum_tensor(f"PM{i}", [128, C], F32) for i in range(4)]

    P = Prog()

    bXIN = [Buf(f"xin{t}") for t in range(NT)]
    bXRES = [Buf(f"xres{t}") for t in range(NT)]
    bXN = [Buf(f"xn{t}") for t in range(NT)]
    bARENA = [Buf(f"ar{h}") for h in range(32)]
    bMIXK = [Buf(f"mk{t}") for t in range(NT)]
    bMIXV = [Buf(f"mv{t}") for t in range(NT)]
    bMIXR = [Buf(f"mr{t}") for t in range(NT)]
    bSIG = [Buf(f"sig{t}") for t in range(NT)]
    bPT = [Buf("pt0"), Buf("pt1")]
    bPVT = Buf("pvt")
    bAL = Buf("al")
    bBE = Buf("be")
    bNUM = Buf("num")
    bSC = [Buf("sc0"), Buf("sc1")]
    bSRT = [Buf("srt0"), Buf("srt1")]
    bFTMP = [Buf("ft0"), Buf("ft1")]
    bTMP1 = Buf("tmp1")
    bTMPD = Buf("tmpd")
    bXB = [Buf(f"xb{i}") for i in range(4)]
    bXSQ = [Buf(f"xsq{i}") for i in range(4)]
    bMU, bM2, bMUSQ = Buf("mu"), Buf("m2"), Buf("musq")
    bSROW, bBROW, bHIF = Buf("srow"), Buf("brow"), Buf("hif")
    bSHI, bSLO, bBHI, bBLO = Buf("shi"), Buf("slo"), Buf("bhi"), Buf("blo")
    bCST = Buf("cst")
    bCDW, bCDN, bCDLAM, bCDEU = Buf("cdw"), Buf("cdn"), Buf("cdlam"), Buf("cdeu")
    bCARA, bCARB, bC1, bC2 = Buf("cara"), Buf("carb"), Buf("c1"), Buf("c2")
    bONESC, bONESR = Buf("onesc"), Buf("onesr")
    bCSTRB = Buf("cstrb")
    bY = [Buf(f"yy{t}") for t in range(NT)]
    bPS1, bPS2, bPA, bPB = Buf("ps1"), Buf("ps2"), Buf("pa"), Buf("pb")
    bPM = [Buf(f"pm{i}") for i in range(4)]
    bTAPE = [Buf(f"tape{j}") for j in range(NPIECE)]
    bSLOT = [Buf(f"slot{s}") for s in range(NS)]
    bYD = Buf("yd")

    def xin_ap(t):   return XIN[t][:]
    def xres_ap(t):  return XRES[t][:]
    def y_ap(t):     return Y[t][:]
    def ar_ap(h):    return ARENA[h][:]
    def mixk_ap(t):  return MIXK[t][:]
    def mixv_ap(t):  return MIXV[t][:]
    def mixr_ap(t):  return MIXR[t][:]
    def sig_ap(t):   return SIG[t][:]
    def xn_ap(t):    return XN[t][:, 1:C + 1]
    def lx_ap(t):    return XN[t][:, 0:C]
    def xn0_ap(t):   return XN[t][:, 0:1]
    def xnL_ap(t):   return XN[t][:, C:C + 1]
    def cst_col(base, t): return CST[:, base + t: base + t + 1]
    LN1W, LN1B, LN2W, LN2B, TDC, TFC = 0, 8, 16, 24, 32, 40
    TMK, TMV, TMR, FTMK, FTMR = 48, 56, 64, 72, 80
    def lam_col(t): return CDLAM[:, t: t + 1]
    def eu_col(t):  return CDEU[:, t: t + 1]

    dcst = P.dma("sp", lambda e: e.dma_start(out=CST[:], in_=cst_d), writes=[bCST])
    P.op("act", lambda e: e.activation(out=CDW8[:], in_=CST[:, TDC:TDC + 8],
                                       func=AF.Exp, bias=0.0, scale=1.0),
         reads=[bCST], writes=[bCDW])
    P.op("dve", lambda e: e.tensor_scalar(out=CDN[:], in0=CDW8[:], scalar1=-1.0,
                                          scalar2=None, op0=OP.mult),
         reads=[bCDW], writes=[bCDN])
    P.op("act", lambda e: e.activation(out=CDLAM[:], in_=CDN[:],
                                       func=AF.Exp, bias=0.0, scale=1.0),
         reads=[bCDN], writes=[bCDLAM])
    P.op("act", lambda e: e.activation(out=CDEU[:], in_=CST[:, TFC:TFC + 8],
                                       func=AF.Exp, bias=0.0, scale=1.0),
         reads=[bCST], writes=[bCDEU])
    P.dma("pool", lambda e: e.dma_start(out=CSTRB[:], in_=cstr_d), writes=[bCSTRB])
    P.op("dve", lambda e: e.memset(ONESC[:], 1.0), writes=[bONESC])
    P.op("dve", lambda e: e.memset(ONESR[:], 1.0), writes=[bONESR])

    NPF = FF_COLS // PIECE
    for j in range(NPIECE):
        if j < NPF:
            src = wtf[:, j * PIECE:(j + 1) * PIECE]
        else:
            src = wti[:, (j - NPF) * PIECE:(j - NPF + 1) * PIECE]
        dst = tapes[j][:]
        P.dma("pool", (lambda s=src, d=dst: (lambda e: e.dma_start(out=d, in_=s)))(),
              writes=[bTAPE[j]])


    tapped = set()

    def tap(label, ap_or_parts, bufs=None):
        if label not in DBG or label in tapped:
            return
        tapped.add(label)
        if bufs is not None:
            parts = [(ap_or_parts, list(bufs))]
        else:
            parts = [(a, list(bb)) for a, bb in ap_or_parts]
        tot = sum(a.shape[1] for a, _ in parts)
        td = nc.dram_tensor(f"tap_{label}", [parts[0][0].shape[0], tot],
                            parts[0][0].dtype, kind="ExternalOutput").ap()
        off = 0
        for a, bb in parts:
            d = td[:, off:off + a.shape[1]]
            off += a.shape[1]
            P.dma("sp", (lambda a_=a, d_=d: (lambda e: e.dma_start(out=d_, in_=a_)))(),
                  reads=bb)

    tap("cst", CST[:], [bCST])
    tap("tape0", tapes[0][:], [bTAPE[0]])
    tap("tape26", tapes[26][:], [bTAPE[26]])

    ring_emitted = {}

    def ensure_ring(g, j):
        for jj in range(j, min(j + PF + 1, NPIECE)):
            if (g, jj) in ring_emitted:
                continue
            ring_emitted[(g, jj)] = True
            gp = g * NPIECE + jj
            s = gp % NS
            dst = RING[s][:]
            src = tapes[jj][:]
            P.dma("sp", (lambda s_=src, d_=dst: (lambda e: e.dma_start(out=d_, in_=s_)))(),
                  reads=[bTAPE[jj]], writes=[bSLOT[s]])

    def ring_lhsT(g, col):
        j = col // PIECE
        gp = g * NPIECE + j
        s = gp % NS
        off = col % PIECE
        return RING[s][:, off: off + 128], bSLOT[s]

    def mm(g, ps, ps_buf, col, rhs_ap, rhs_buf, start, stop):
        ensure_ring(g, col // PIECE)
        la, sbuf_ = ring_lhsT(g, col)
        reads = [sbuf_, rhs_buf]
        writes = [ps_buf] if (start or stop) else []
        if not start:
            reads = reads + [ps_buf]
        return P.op("pe",
                    (lambda la_=la, r_=rhs_ap, p_=ps, st=start, sp_=stop:
                     (lambda e: e.matmul(p_[:], la_, r_, start=st, stop=sp_,
                                         skip_group_check=True)))(),
                    reads=reads, writes=writes)

    def ln_stat_cast(t, src_one, src_buf):
        r = t % 4
        P.op("dve", (lambda r_=r, s_=src_one: (lambda e: e.tensor_copy(
            out=XB[r_][:], in_=s_)))(),
             reads=[src_buf], writes=[bXB[r]], big=True)
        P.op("act", (lambda r_=r, s_=src_one: (lambda e: e.activation(
            out=XSQ[r_][:], in_=s_, func=AF.Square, bias=0.0, scale=1.0)))(),
             reads=[src_buf], writes=[bXSQ[r]], big=True)

    def ln_stat_mm(t):
        r = t % 4
        P.op("pe", (lambda t_=t, r_=r: (lambda e: e.matmul(
            PS1[:], ONESC[:], XB[r_][:], start=(t_ == 0), stop=(t_ == NT - 1),
            skip_group_check=True)))(),
             reads=[bXB[r], bONESC] + ([bPS1] if t else []),
             writes=[bPS1] if (t == 0 or t == NT - 1) else [])
        P.op("pe", (lambda t_=t, r_=r: (lambda e: e.matmul(
            PS2[:], ONESC[:], XSQ[r_][:], start=(t_ == 0), stop=(t_ == NT - 1),
            skip_group_check=True)))(),
             reads=[bXSQ[r], bONESC] + ([bPS2] if t else []),
             writes=[bPS2] if (t == 0 or t == NT - 1) else [])

    def ln_rows():
        MU, M2, MUSQ = R_MU[:], R_M2[:], R_MUSQ[:]
        SROW, BROW = R_S[:], R_B[:]
        P.op("dve", lambda e: e.tensor_scalar(out=MU, in0=PS1[:], scalar1=1.0 / D,
                                              scalar2=None, op0=OP.mult),
             reads=[bPS1], writes=[bMU])
        P.op("act", lambda e: e.activation(out=M2, in_=PS2[:], func=AF.Identity,
                                           bias=0.0, scale=1.0 / D),
             reads=[bPS2], writes=[bM2])
        P.op("dve", lambda e: e.tensor_tensor(out=MUSQ, in0=MU, in1=MU, op=OP.mult),
             reads=[bMU], writes=[bMUSQ])
        P.op("dve", lambda e: e.scalar_tensor_tensor(out=M2, in0=M2, scalar=EPS,
                                                     in1=MUSQ, op0=OP.add,
                                                     op1=OP.subtract),
             reads=[bM2, bMUSQ], writes=[bM2])
        P.op("act", lambda e: e.activation(out=M2, in_=M2, func=AF.Ln,
                                           bias=0.0, scale=1.0),
             reads=[bM2], writes=[bM2])
        P.op("act", lambda e: e.activation(out=SROW, in_=M2, func=AF.Exp,
                                           bias=0.0, scale=-0.5),
             reads=[bM2], writes=[bSROW])
        P.op("dve", lambda e: e.tensor_copy(out=R_HI[:], in_=SROW),
             reads=[bSROW], writes=[bSHI])
        P.op("dve", lambda e: e.tensor_tensor(out=BROW, in0=MU, in1=SROW, op=OP.mult),
             reads=[bMU, bSROW], writes=[bBROW])
        P.op("dve", lambda e: e.tensor_copy(out=R_LO[:], in_=BROW),
             reads=[bBROW], writes=[bSLO])

    def outer_bcast():
        P.op("pe", lambda e: e.matmul(PA[:], ONESR[:], R_HI[:], start=True,
                                      stop=True, skip_group_check=True),
             reads=[bSHI, bONESR], writes=[bPA])
        P.op("pe", lambda e: e.matmul(PB[:], ONESR[:], R_LO[:], start=True,
                                      stop=True, skip_group_check=True),
             reads=[bSLO, bONESR], writes=[bPB])

    PAIR_AB = (((PA, PB), (bPA, bPB)),)
    PAIR_AB23 = (((PA, PB), (bPA, bPB)), ((PM[2], PM[3]), (bPM[2], bPM[3])))
    PAIR_3 = (((PA, PB), (bPA, bPB)), ((PM[2], PM[3]), (bPM[2], bPM[3])),
              ((PM[0], PM[1]), (bPM[0], bPM[1])))

    def apply_tile(t, ci_this, wbase, bbase, src_ap_, src_bufs, carry, carrybuf,
                   pairs):
        pa_t, pb_t = pairs[t % len(pairs)][0]
        bpa_t, bpb_t = pairs[t % len(pairs)][1]
        if True:
            wrow = CSTRB[0:1, wbase + t * 128: wbase + (t + 1) * 128]
            P.op("pe", (lambda w_=wrow, p_=pa_t: (lambda e: e.matmul(
                p_[:], w_, R_HI[:], start=True, stop=True,
                skip_group_check=True)))(),
                 reads=[bSHI, bCSTRB], writes=[bpa_t])
            P.op("pe", (lambda w_=wrow, p_=pb_t: (lambda e: e.matmul(
                p_[:], w_, R_LO[:], start=True, stop=True,
                skip_group_check=True)))(),
                 reads=[bSLO, bCSTRB], writes=[bpb_t])
            if ci_this == 0:
                P.op("dve", (lambda t_=t: (lambda e: e.memset(xn0_ap(t_), 0.0)))(),
                     writes=[bXN[t]])
            else:
                P.op("dve", (lambda t_=t, cb=carry: (lambda e: e.tensor_copy(
                    out=xn0_ap(t_), in_=cb[:, t_:t_ + 1])))(),
                     reads=[carrybuf], writes=[bXN[t]])
            P.op("dve", (lambda t_=t, p_=pa_t, bp=bpa_t: (lambda e: e.tensor_tensor(
                out=TMP1[:], in0=src_ap_(t_), in1=p_[:], op=OP.mult)))(),
                 reads=[src_bufs[t], bpa_t], writes=[bTMP1], big=True)
            P.op("dve", (lambda t_=t, p_=pb_t, bb=bbase: (lambda e: e.scalar_tensor_tensor(
                out=xn_ap(t_), in0=TMP1[:], scalar=cst_col(bb, t_), in1=p_[:],
                op0=OP.add, op1=OP.subtract)))(),
                 reads=[bTMP1, bpb_t, bCST], writes=[bXN[t]], big=True)

    def mix_tile(t, mixes):
        P.op("dve", (lambda t_=t: (lambda e: e.tensor_tensor(
            out=TMPD[:], in0=xn_ap(t_), in1=lx_ap(t_), op=OP.subtract)))(),
             reads=[bXN[t]], writes=[bTMPD], big=True)
        for base, MAP, mb in mixes:
            P.op("dve", (lambda t_=t, b_=base, m_=MAP: (lambda e: e.scalar_tensor_tensor(
                out=m_(t_), in0=TMPD[:], scalar=cst_col(b_, t_), in1=lx_ap(t_),
                op0=OP.mult, op1=OP.add)))(),
                 reads=[bTMPD, bXN[t], bCST], writes=[mb[t]], big=True)

    def carry_saves(carry, carrybuf):
        for t in range(NT):
            P.op("dve", (lambda t_=t, cb=carry: (lambda e: e.tensor_copy(
                out=cb[:, t_:t_ + 1], in_=xnL_ap(t_))))(),
                 reads=[bXN[t]], writes=[carrybuf])

    def inner_apply(ci_this, wbase, bbase, src_ap_, src_bufs, carry, carrybuf,
                    pairs):
        for t in range(NT):
            apply_tile(t, ci_this, wbase, bbase, src_ap_, src_bufs, carry,
                       carrybuf, pairs)

    def inner_mixes(mixes, carry, carrybuf):
        for t in range(NT):
            mix_tile(t, mixes)
        carry_saves(carry, carrybuf)

    xd_r = xd.rearrange("(b t p) c -> b t p c", b=B_CORE, t=NT, p=128)
    yd_r = yd.rearrange("(b t p) c -> b t p c", b=B_CORE, t=NT, p=128)

    def emit_x_load(gq):
        bq, ciq = gq // NCH, gq % NCH
        for t in range(NT):
            srcx = xd_r[bq][t][:, ciq * C: ciq * C + C]
            P.dma("sp", (lambda s_=srcx, t_=t: (lambda e: e.dma_start(
                out=XIN[t_][:], in_=s_)))(),
                  writes=[bXIN[t]])

    LN1_MIXES = ((TMK, mixk_ap, bMIXK), (TMV, mixv_ap, bMIXV),
                 (TMR, mixr_ap, bMIXR))
    LN2_MIXES = ((FTMK, mixv_ap, bMIXV), (FTMR, mixr_ap, bMIXR))

    emit_x_load(0)
    for t in range(NT):
        ln_stat_cast(t, xin_ap(t), bXIN[t])
        ln_stat_mm(t)
    ln_rows()
    inner_apply(0, 0, LN1B, xin_ap, bXIN, C1, bC1, PAIR_AB23)
    inner_mixes(LN1_MIXES, C1, bC1)

    NG = B_CORE * NCH
    pending_ln4 = [None]

    for g in range(NG):
        b, ci = g // NCH, g % NCH
        c0 = ci * C

        for t in range(NT):
            kb, bkb = PM[t % 2], bPM[t % 2]
            vb, bvb = (PM[2], bPM[2]) if t % 2 == 0 else (PA, bPA)
            rb, brb = (PM[3], bPM[3]) if t % 2 == 0 else (PB, bPB)
            for kt in range(NT):
                mm(g, kb, bkb, col_kvr(0, t, kt), mixk_ap(kt), bMIXK[kt],
                   kt == 0, kt == NT - 1)
            for kt in range(NT):
                mm(g, vb, bvb, col_kvr(1, t, kt), mixv_ap(kt), bMIXV[kt],
                   kt == 0, kt == NT - 1)
            for kt in range(NT):
                mm(g, rb, brb, col_kvr(2, t, kt), mixr_ap(kt), bMIXR[kt],
                   kt == 0, kt == NT - 1)
            if t == 2 and pending_ln4[0] is not None:
                pending_ln4[0]()
                pending_ln4[0] = None
            r = t % 2
            P.op("act", (lambda r_=r, k_=kb: (lambda e: e.activation(
                out=PT[r_][:], in_=k_[:], func=AF.Exp, bias=0.0, scale=1.0)))(),
                 reads=[bkb], writes=[bPT[r]], big=True)
            P.op("dve", (lambda r_=r, v_=vb: (lambda e: e.tensor_tensor(
                out=PVT[:], in0=PT[r_][:], in1=v_[:], op=OP.mult)))(),
                 reads=[bPT[r], bvb], writes=[bPVT], big=True)
            P.op("act", (lambda r_=r, rr=rb: (lambda e: e.activation(
                out=SRT[r_][:], in_=rr[:], func=AF.Exp, bias=0.0, scale=-1.0)))(),
                 reads=[brb], writes=[bSRT[r]], big=True)
            ia = 0.0 if ci == 0 else CARA[:, t:t + 1]
            ib = 0.0 if ci == 0 else CARB[:, t:t + 1]
            car_reads = [] if ci == 0 else [bCARA]
            cbr_reads = [] if ci == 0 else [bCARB]
            P.op("dve", (lambda t_=t, i_=ia: (lambda e: e.tensor_tensor_scan(
                out=AL[:, 1:C + 1], data0=lam_col(t_).broadcast_to([128, C]),
                data1=PVT[:], initial=i_, op0=OP.mult, op1=OP.add)))(),
                 reads=[bPVT, bCDLAM] + car_reads, writes=[bAL], big=True)
            P.op("dve", (lambda t_=t, r_=r, i_=ib: (lambda e: e.tensor_tensor_scan(
                out=BE[:, 1:C + 1], data0=lam_col(t_).broadcast_to([128, C]),
                data1=PT[r_][:], initial=i_, op0=OP.mult, op1=OP.add)))(),
                 reads=[bPT[r], bCDLAM] + cbr_reads, writes=[bBE], big=True)
            if ci == 0:
                P.op("dve", (lambda t_=t: (lambda e: e.tensor_scalar(
                    out=NUM[:, 0:1], in0=PVT[:, 0:1], scalar1=eu_col(t_),
                    scalar2=None, op0=OP.mult)))(),
                     reads=[bPVT, bCDEU], writes=[bNUM])
            else:
                P.op("dve", (lambda t_=t: (lambda e: e.scalar_tensor_tensor(
                    out=NUM[:, 0:1], in0=PVT[:, 0:1], scalar=eu_col(t_),
                    in1=CARA[:, t_:t_ + 1], op0=OP.mult, op1=OP.add)))(),
                     reads=[bPVT, bCDEU, bCARA], writes=[bNUM])
            P.op("dve", (lambda t_=t: (lambda e: e.scalar_tensor_tensor(
                out=NUM[:, 1:C], in0=PVT[:, 1:C], scalar=eu_col(t_),
                in1=AL[:, 1:C], op0=OP.mult, op1=OP.add)))(),
                 reads=[bPVT, bCDEU, bAL], writes=[bNUM], big=True)
            if ci == 0:
                P.op("dve", (lambda t_=t, r_=r: (lambda e: e.tensor_scalar(
                    out=SC[r_][:, 0:1], in0=PT[r_][:, 0:1], scalar1=eu_col(t_),
                    scalar2=None, op0=OP.mult)))(),
                     reads=[bPT[r], bCDEU], writes=[bSC[r]])
            else:
                P.op("dve", (lambda t_=t, r_=r: (lambda e: e.scalar_tensor_tensor(
                    out=SC[r_][:, 0:1], in0=PT[r_][:, 0:1], scalar=eu_col(t_),
                    in1=CARB[:, t_:t_ + 1], op0=OP.mult, op1=OP.add)))(),
                     reads=[bPT[r], bCDEU, bCARB], writes=[bSC[r]])
            P.op("dve", (lambda t_=t, r_=r: (lambda e: e.scalar_tensor_tensor(
                out=SC[r_][:, 1:C], in0=PT[r_][:, 1:C], scalar=eu_col(t_),
                in1=BE[:, 1:C], op0=OP.mult, op1=OP.add)))(),
                 reads=[bPT[r], bCDEU, bBE], writes=[bSC[r]], big=True)
            P.op("dve", (lambda t_=t: (lambda e: e.tensor_copy(
                out=CARA[:, t_:t_ + 1], in_=AL[:, C:C + 1])))(),
                 reads=[bAL], writes=[bCARA])
            P.op("dve", (lambda t_=t: (lambda e: e.tensor_copy(
                out=CARB[:, t_:t_ + 1], in_=BE[:, C:C + 1])))(),
                 reads=[bBE], writes=[bCARB])
            P.op("act", (lambda r_=r: (lambda e: e.activation(
                out=SC[r_][:], in_=SC[r_][:], func=AF.Ln, bias=0.0, scale=1.0)))(),
                 reads=[bSC[r]], writes=[bSC[r]], big=True)
            P.op("act", (lambda r_=r: (lambda e: e.activation(
                out=SC[r_][:], in_=SC[r_][:], func=AF.Exp, bias=0.0, scale=-1.0)))(),
                 reads=[bSC[r]], writes=[bSC[r]], big=True)
            P.op("act", (lambda r_=r: (lambda e: e.activation(
                out=SRT[r_][:], in_=SRT[r_][:], func=AF.Ln, bias=1.0, scale=1.0)))(),
                 reads=[bSRT[r]], writes=[bSRT[r]], big=True)
            P.op("act", (lambda r_=r: (lambda e: e.activation(
                out=SRT[r_][:], in_=SRT[r_][:], func=AF.Exp, bias=0.0, scale=-1.0)))(),
                 reads=[bSRT[r]], writes=[bSRT[r]], big=True)
            P.op("dve", (lambda r_=r: (lambda e: e.tensor_tensor(
                out=NUM[:], in0=NUM[:], in1=SC[r_][:], op=OP.mult)))(),
                 reads=[bNUM, bSC[r]], writes=[bNUM], big=True)
            P.op("dve", (lambda t_=t, r_=r: (lambda e: e.tensor_tensor(
                out=ar_ap(t_), in0=NUM[:], in1=SRT[r_][:], op=OP.mult)))(),
                 reads=[bNUM, bSRT[r]], writes=[bARENA[t]], big=True)

        for gg in range(4):
            ps, bps = PM[gg % 4], bPM[gg % 4]
            for kt in range(NT - 1):
                mm(g, ps, bps, col_wout(gg, kt), ar_ap(kt), bARENA[kt],
                   kt == 0, False)
        for gg in range(NT):
            ps, bps = PM[gg % 4], bPM[gg % 4]
            if gg < 4:
                mm(g, ps, bps, col_wout(gg, NT - 1), ar_ap(NT - 1),
                   bARENA[NT - 1], False, True)
            else:
                for kt in range(NT):
                    mm(g, ps, bps, col_wout(gg, kt), ar_ap(kt), bARENA[kt],
                       kt == 0, kt == NT - 1)
            P.op("dve", (lambda g_=gg, p_=ps: (lambda e: e.tensor_tensor(
                out=xres_ap(g_), in0=xin_ap(g_), in1=p_[:], op=OP.add)))(),
                 reads=[bXIN[gg], bps], writes=[bXRES[gg]], big=True)
            ln_stat_cast(gg, xres_ap(gg), bXRES[gg])
            if gg >= 2:
                ln_stat_mm(gg - 2)
        ln_stat_mm(NT - 2)
        ln_stat_mm(NT - 1)

        ln_rows()
        inner_apply(ci, D, LN2B, xres_ap, bXRES, C2, bC2, PAIR_3)
        inner_mixes(LN2_MIXES, C2, bC2)

        for gg in range(NT):
            ps, bps = PM[2 + gg % 2], bPM[2 + gg % 2]
            for kt in range(NT):
                mm(g, ps, bps, col_frec(gg, kt), mixr_ap(kt), bMIXR[kt],
                   kt == 0, kt == NT - 1)
            P.op("act", (lambda g_=gg, p_=ps: (lambda e: e.activation(
                out=sig_ap(g_), in_=p_[:], func=AF.Exp, bias=0.0, scale=-1.0)))(),
                 reads=[bps], writes=[bSIG[gg]], big=True)

        for h in range(32):
            ps, bps = PM[h % 4], bPM[h % 4]
            for kt in range(NT):
                mm(g, ps, bps, col_fkey(h, kt), mixv_ap(kt), bMIXV[kt],
                   kt == 0, kt == NT - 1)
            P.op("act", (lambda h_=h, p_=ps: (lambda e: e.activation(
                out=ar_ap(h_), in_=p_[:], func=AF.Relu, bias=0.0, scale=1.0)))(),
                 reads=[bps], writes=[bARENA[h]], big=True)
            P.op("dve", (lambda h_=h: (lambda e: e.tensor_tensor(
                out=ar_ap(h_), in0=ar_ap(h_), in1=ar_ap(h_), op=OP.mult)))(),
                 reads=[bARENA[h]], writes=[bARENA[h]], big=True)
            if h < 2 * NT and h % 2 == 1:
                gg2 = h // 2
                P.op("act", (lambda g_=gg2: (lambda e: e.activation(
                    out=sig_ap(g_), in_=sig_ap(g_), func=AF.Ln, bias=1.0, scale=1.0)))(),
                     reads=[bSIG[gg2]], writes=[bSIG[gg2]], big=True)
                P.op("act", (lambda g_=gg2: (lambda e: e.activation(
                    out=sig_ap(g_), in_=sig_ap(g_), func=AF.Exp, bias=0.0, scale=-1.0)))(),
                     reads=[bSIG[gg2]], writes=[bSIG[gg2]], big=True)

        for dd in range(NT):
            ps, bps = PM[dd % 4], bPM[dd % 4]
            for th in range(32):
                mm(g, ps, bps, col_fval(dd, th), ar_ap(th), bARENA[th],
                   th == 0, th == 31)
            P.op("dve", (lambda d_=dd, p_=ps: (lambda e: e.tensor_tensor(
                out=TMP1[:], in0=sig_ap(d_), in1=p_[:], op=OP.mult)))(),
                 reads=[bSIG[dd], bps], writes=[bTMP1], big=True)
            P.op("dve", (lambda d_=dd: (lambda e: e.tensor_tensor(
                out=xres_ap(d_), in0=xres_ap(d_), in1=TMP1[:], op=OP.add)))(),
                 reads=[bXRES[dd], bTMP1], writes=[bXRES[dd]], big=True)
            P.op("dve", (lambda t_=dd: (lambda e: e.tensor_tensor(
                out=xres_ap(t_), in0=xres_ap(t_), in1=xin_ap(t_), op=OP.add)))(),
                 reads=[bXRES[dd], bXIN[dd]], writes=[bXRES[dd]], big=True)
            ln_stat_cast(dd, xres_ap(dd), bXRES[dd])
            if dd >= 2:
                ln_stat_mm(dd - 2)
        ln_stat_mm(NT - 2)
        ln_stat_mm(NT - 1)

        ln_rows()
        outer_bcast()
        for t in range(NT):
            P.op("dve", (lambda t_=t: (lambda e: e.tensor_tensor(
                out=TMP1[:], in0=xres_ap(t_), in1=PA[:], op=OP.mult)))(),
                 reads=[bXRES[t], bPA], writes=[bTMP1], big=True)
            P.op("dve", (lambda t_=t: (lambda e: e.tensor_tensor(
                out=y_ap(t_), in0=TMP1[:], in1=PB[:], op=OP.subtract)))(),
                 reads=[bTMP1, bPB], writes=[bY[t]], big=True)
            P.op("dve", (lambda t_=t: (lambda e: e.tensor_copy(
                out=ar_ap(24 + t_), in_=y_ap(t_))))(),
                 reads=[bY[t]], writes=[bARENA[24 + t]], big=True)

        if g + 1 < NG:
            emit_x_load(g + 1)

        for h in range(24):
            gp, bgp = PM[h % 2], bPM[h % 2]
            up, bup = PM[2 + h % 2], bPM[2 + h % 2]
            for kt in range(NT):
                mm(g, gp, bgp, col_gu(0, h, kt), ar_ap(24 + kt), bARENA[24 + kt],
                   kt == 0, kt == NT - 1)
            for kt in range(NT):
                mm(g, up, bup, col_gu(1, h, kt), ar_ap(24 + kt), bARENA[24 + kt],
                   kt == 0, kt == NT - 1)
            r = h % 2
            P.op("dve", (lambda r_=r, p_=gp: (lambda e: e.tensor_scalar_max(
                out=FTMP[r_][:], in0=p_[:], scalar1=-5000.0)))(),
                 reads=[bgp], writes=[bFTMP[r]], big=True)
            P.op("act", (lambda r_=r: (lambda e: e.activation(
                out=FTMP[r_][:], in_=FTMP[r_][:], func=AF.Exp, bias=0.0,
                scale=-1.0 / 127.0)))(),
                 reads=[bFTMP[r]], writes=[bFTMP[r]], big=True)
            P.op("act", (lambda r_=r: (lambda e: e.activation(
                out=FTMP[r_][:], in_=FTMP[r_][:], func=AF.Ln, bias=1.0, scale=1.0)))(),
                 reads=[bFTMP[r]], writes=[bFTMP[r]], big=True)
            P.op("act", (lambda r_=r: (lambda e: e.activation(
                out=FTMP[r_][:], in_=FTMP[r_][:], func=AF.Exp, bias=0.0, scale=-1.0)))(),
                 reads=[bFTMP[r]], writes=[bFTMP[r]], big=True)
            P.op("dve", (lambda r_=r, p_=gp: (lambda e: e.tensor_tensor(
                out=FTMP[r_][:], in0=FTMP[r_][:], in1=p_[:], op=OP.mult)))(),
                 reads=[bFTMP[r], bgp], writes=[bFTMP[r]], big=True)
            P.op("dve", (lambda r_=r, h_=h, p_=up: (lambda e: e.scalar_tensor_tensor(
                out=ar_ap(h_), in0=FTMP[r_][:], scalar=1.0 / (127.0 * 127.0),
                in1=p_[:], op0=OP.mult, op1=OP.mult)))(),
                 reads=[bFTMP[r], bup], writes=[bARENA[h]], big=True)
            if g + 1 < NG and 8 <= h < 16:
                ln_stat_cast(h - 8, xin_ap(h - 8), bXIN[h - 8])
            if g + 1 < NG and 11 <= h < 19:
                ln_stat_mm(h - 11)
        if g + 1 < NG:
            ln_rows()

        for dd in range(NT):
            ps, bps = PM[dd % 4], bPM[dd % 4]
            for th in range(24):
                mm(g, ps, bps, col_down(dd, th), ar_ap(th), bARENA[th],
                   th == 0, th == 23)
            P.op("dve", (lambda d_=dd, p_=ps: (lambda e: e.scalar_tensor_tensor(
                out=xres_ap(d_), in0=p_[:], scalar=1.0 / 127.0, in1=y_ap(d_),
                op0=OP.mult, op1=OP.add)))(),
                 reads=[bY[dd], bps], writes=[bXRES[dd]], big=True)
            ln_stat_cast(dd, xres_ap(dd), bXRES[dd])
            if dd >= 2:
                ln_stat_mm(dd - 2)
            if g + 1 < NG:
                apply_tile(dd, (g + 1) % NCH, 0, LN1B, xin_ap, bXIN, C1, bC1,
                           PAIR_AB)
                mix_tile(dd, LN1_MIXES)
        ln_stat_mm(NT - 2)
        ln_stat_mm(NT - 1)
        if g + 1 < NG:
            carry_saves(C1, bC1)
        ln_rows()

        def ln4_finish(b_=b, c0_=c0):
            outer_bcast()
            for t in range(NT):
                P.op("dve", (lambda t_=t: (lambda e: e.tensor_tensor(
                    out=TMP1[:], in0=xres_ap(t_), in1=PA[:], op=OP.mult)))(),
                     reads=[bXRES[t], bPA], writes=[bTMP1], big=True)
                P.op("dve", (lambda t_=t: (lambda e: e.tensor_tensor(
                    out=y_ap(t_), in0=TMP1[:], in1=PB[:], op=OP.subtract)))(),
                     reads=[bTMP1, bPB], writes=[bY[t]], big=True)
            for t in range(NT):
                dstp = yd_r[b_][t][:, c0_:c0_ + C]
                P.dma("sp", (lambda d_=dstp, t_=t: (lambda e: e.dma_start(
                    out=d_, in_=y_ap(t_))))(),
                      reads=[bY[t]], writes=[bYD])

        if g + 1 < NG:
            ensure_ring(g + 1, 0)
            pending_ln4[0] = ln4_finish
        else:
            ln4_finish()

    es = ExitStack()
    esem = {e: es.enter_context(nc.semaphore(f"sem_{e}"))
            for e in ("pe", "dve", "act", "pool")}
    esem["sp"] = None
    dsem = {q: [es.enter_context(nc.semaphore(f"dma_{q}{i}")) for i in range(NDMASEM)]
            for q in ("sp", "pool")}
    block = es.enter_context(nc.Block())
    P.finalize(nc, block, esem, dsem)
    es.close()
    return nc


def _tape_layout(w):
    din, dout = w.shape
    ti, to = din // 128, dout // 128
    return np.ascontiguousarray(
        w.reshape(ti, 128, to, 128).transpose(1, 2, 0, 3).reshape(128, -1))


def _build_tapes(i):
    kl = _tape_layout(i["L0_attn_w_key"])
    vl = _tape_layout(i["L0_attn_w_value"])
    rl = _tape_layout(i["L0_attn_w_recept"])
    kvr = np.stack([kl.reshape(128, 8, 1024), vl.reshape(128, 8, 1024),
                    rl.reshape(128, 8, 1024)], axis=2).reshape(128, -1)
    tape_f = np.concatenate([
        kvr,
        _tape_layout(i["L0_attn_w_output"]),
        _tape_layout(i["L0_attn_f_w_recept"]),
        _tape_layout(i["L0_attn_f_w_key"]),
        _tape_layout(i["L0_attn_f_w_value"]),
    ], axis=1)
    gl = _tape_layout(np.ascontiguousarray(i["L0_ffn_w_gate"].T))
    ul = _tape_layout(np.ascontiguousarray(i["L0_ffn_w_up"].T))
    gu = np.stack([gl.reshape(128, 24, 1024), ul.reshape(128, 24, 1024)],
                  axis=2).reshape(128, -1)
    tape_i = np.concatenate([
        gu,
        _tape_layout(np.ascontiguousarray(i["L0_ffn_w_down"].T)),
    ], axis=1)
    tape_f = np.ascontiguousarray(tape_f, dtype=np.float32)
    tape_i = np.ascontiguousarray(tape_i, dtype=np.int32)
    assert tape_f.shape[1] == FF_COLS and tape_i.shape[1] == TC - FF_COLS
    return tape_f, tape_i


def _build_consts(i):
    def cols(v):
        return np.asarray(v).reshape(8, 128).T
    order = ["L0_attn_ln1_w", "L0_attn_ln1_b", "L0_attn_ln2_w", "L0_attn_ln2_b",
             "L0_attn_time_decay", "L0_attn_time_first", "L0_attn_time_mix_k",
             "L0_attn_time_mix_v", "L0_attn_time_mix_r", "L0_attn_f_time_mix_k",
             "L0_attn_f_time_mix_r"]
    cst = np.ascontiguousarray(
        np.concatenate([cols(i[k]) for k in order], axis=1), dtype=np.float32)
    cstr = np.ascontiguousarray(
        np.concatenate([np.asarray(i["L0_attn_ln1_w"]).ravel(),
                        np.asarray(i["L0_attn_ln2_w"]).ravel()]).reshape(1, -1),
        dtype=np.float32)
    return cst, cstr


def kernel(**inputs):
    x = np.asarray(inputs["L0_attn_x"], dtype=np.float32)
    xt = np.ascontiguousarray(x.transpose(0, 2, 1))
    tape_f, tape_i = _build_tapes(inputs)
    consts, cstr = _build_consts(inputs)

    nc = build_nc()
    in_maps = []
    for c in range(8):
        shard = np.ascontiguousarray(
            xt[c * B_CORE:(c + 1) * B_CORE].reshape(B_CORE * D, T))
        in_maps.append({"xin": shard, "wtf": tape_f, "wti": tape_i,
                        "consts": consts, "cstr": cstr})
    res = run_bass_kernel_spmd(nc, in_maps, core_ids=list(range(8)))
    outs = []
    for c in range(8):
        yc = np.asarray(res.results[c]["y"]).reshape(B_CORE, D, T)
        outs.append(yc.transpose(0, 2, 1))
    return np.ascontiguousarray(np.concatenate(outs, axis=0), dtype=np.float32)
